# version 03
# baseline (speedup 1.0000x reference)
import numpy as np
import ml_dtypes
import concourse.bass as bass
import concourse.mybir as mybir
from concourse.bass_utils import run_bass_kernel_spmd

F32 = mybir.dt.float32
BF16 = mybir.dt.bfloat16
AF = mybir.ActivationFunctionType
ALU = mybir.AluOpType
AX = mybir.AxisListType

NCORES = 8
L = 8192
D = 4096
ROWS = L // NCORES
NH = 16
FF = 16384
CD = 32
EPS = 1e-5
SCALE = 1.0 / 16.0

def _build_tables():
    g = [1.0, 1.0, 1.0]
    dim, n = 3, 8
    masks = sorted(range(n), key=lambda m: (bin(m).count('1'), m))
    inv = {m: i for i, m in enumerate(masks)}
    C = np.zeros((n, n, n), dtype=np.float64)
    for i, a in enumerate(masks):
        for j, b in enumerate(masks):
            sw, at = 0, a >> 1
            while at:
                sw += bin(at & b).count('1')
                at >>= 1
            s = -1.0 if sw & 1 else 1.0
            for t in range(dim):
                if a >> t & 1 and b >> t & 1:
                    s *= g[t]
            C[i, j, inv[a ^ b]] = s
    grades = np.array([bin(m).count('1') for m in masks])
    rev = np.where(grades * (grades - 1) // 2 % 2 == 1, -1.0, 1.0)
    return C, rev

_C8, _REV = _build_tables()
_BIV = [4, 5, 6]

def _bd_consts():
    I32 = np.eye(CD)
    out = [np.kron(I32, SCALE * np.diag(_REV))]
    for c in range(3):
        out.append(np.kron(I32, _REV[:, None] * _C8[:, :, _BIV[c]]))
    return [b.astype(np.float32) for b in out]

W_BLK = 128 * 512
MAT_SZ = 32 * W_BLK
AG1_N = 3 * MAT_SZ
W1_SZ = 512 * FF
W2_SZ = 2048 * D
AG2_N = W1_SZ + W2_SZ
KT_SZ = D * ROWS
VROW = NH * 257
V_SZ = ROWS * VROW
KMAX_OFF = KT_SZ + V_SZ
AG3_N = KMAX_OFF + 16


PHASES = ["consts", "prep", "ph1", "vproj", "kproj", "qproj", "cbias", "attn", "ffn"]

def build_nc(dbg=(), stop=None):
    dbg = set(dbg)
    _stop_i = PHASES.index(stop) if stop else len(PHASES) - 1
    def want(name):
        return PHASES.index(name) <= _stop_i
    nc = bass.Bass("TRN2", target_bir_lowering=False)

    x_rows = nc.dram_tensor("x_rows", [ROWS, D], F32, kind="ExternalInput").ap()
    wq_s = nc.dram_tensor("wq_s", [D, 512], F32, kind="ExternalInput").ap()
    wk_s = nc.dram_tensor("wk_s", [D, 512], F32, kind="ExternalInput").ap()
    wv_s = nc.dram_tensor("wv_s", [D, 512], F32, kind="ExternalInput").ap()
    w1_s = nc.dram_tensor("w1_s", [512, FF], F32, kind="ExternalInput").ap()
    w2_s = nc.dram_tensor("w2_s", [2048, D], F32, kind="ExternalInput").ap()
    bqt_d = nc.dram_tensor("bqt", [128, 32], F32, kind="ExternalInput").ap()
    bkt_d = nc.dram_tensor("bkt", [128, 32], F32, kind="ExternalInput").ap()
    bvt_d = nc.dram_tensor("bvt", [128, 32], F32, kind="ExternalInput").ap()
    b1t_d = nc.dram_tensor("b1t", [128, 128], F32, kind="ExternalInput").ap()
    b2_d = nc.dram_tensor("b2row", [1, D], F32, kind="ExternalInput").ap()
    bw_d = nc.dram_tensor("bwrow", [1, 48], F32, kind="ExternalInput").ap()
    y_rows = nc.dram_tensor("y_rows", [ROWS, D], F32, kind="ExternalOutput").ap()

    ag1_in = nc.dram_tensor("ag1_in", [1, AG1_N], BF16, kind="Internal").ap()
    ag1_out = nc.dram_tensor("ag1_out", [NCORES, AG1_N], BF16, kind="Internal", addr_space="Shared").ap()
    ag2_in = nc.dram_tensor("ag2_in", [1, AG2_N], BF16, kind="Internal").ap()
    ag2_out = nc.dram_tensor("ag2_out", [NCORES, AG2_N], BF16, kind="Internal", addr_space="Shared").ap()
    ag3_in = nc.dram_tensor("ag3_in", [1, AG3_N], BF16, kind="Internal").ap()
    ag3_out = nc.dram_tensor("ag3_out", [NCORES, AG3_N], BF16, kind="Internal", addr_space="Shared").ap()
    z_spill = nc.dram_tensor("z_spill", [ROWS, D], F32, kind="Internal").ap()
    x3t_spill = nc.dram_tensor("x3t_spill", [D, ROWS], BF16, kind="Internal").ap()
    qm_bounce = nc.dram_tensor("qm_bounce", [16, 1], F32, kind="Internal").ap()
    out_spill = nc.dram_tensor("out_spill", [ROWS, D], F32, kind="Internal").ap()

    dbg_outs = {}
    def dbg_tensor(name, shape, dt):
        dbg_outs[name] = nc.dram_tensor("o_" + name, shape, dt, kind="ExternalOutput").ap()
        return dbg_outs[name]

    ident_i = nc.inline_tensor(np.eye(128, dtype=ml_dtypes.bfloat16), name="identc").ap()
    eh_np = np.zeros((128, 256), dtype=ml_dtypes.bfloat16)
    for h in range(16):
        eh_np[:, h * 16 + h] = 1.0
    eh_i = nc.inline_tensor(np.ascontiguousarray(eh_np), name="ehc").ap()
    ones1f_i = nc.inline_tensor(np.ones((1, 128), dtype=np.float32), name="ones1fc").ap()
    bd_i = []
    for i, b in enumerate(_bd_consts()):
        m = np.concatenate([b[:128, :], b[128:, :]], axis=1).astype(np.float32)
        bd_i.append(nc.inline_tensor(np.ascontiguousarray(m), name=f"bdc{i}").ap())

    base = (nc.sbuf_base + 31) & ~31
    def at(name, off, shape, dt):
        return nc.alloc_sbuf_tensor_at(name, shape, dt, offset=base + off).ap()

    ident = at("ident", 0, [128, 128], BF16)
    eh = at("eh", 256, [128, 256], BF16)
    bqt = at("bqt_s", 768, [128, 32], F32)
    bkt = at("bkt_s", 896, [128, 32], F32)
    bvt = at("bvt_s", 1024, [128, 32], F32)
    b1t = at("b1t_s", 1152, [128, 128], F32)
    ones1f = at("ones1f", 1664, [1, 128], F32)
    cbias = at("cbias", 2176, [128, 16], F32)
    kall = at("kall", 2240, [1, 128], F32)
    km2 = at("km2", 2752, [1, 16], F32)
    qm2 = at("qm2", 2816, [16, 1], F32)
    qmT = at("qmT", 2848, [1, 16], F32)
    csc1 = at("csc1", 2912, [1, 16], F32)
    csc2 = at("csc2", 2976, [1, 16], F32)
    stats = at("stats", 3040, [128, 48], F32)
    aggr = at("aggr", 3232, [128, 2], F32)
    rstd = at("rstd", 3264, [128, 1], F32)
    t128a = at("t128a", 3296, [128, 1], F32)
    t128b = at("t128b", 3328, [128, 1], F32)
    epsc = at("epsc", 3360, [128, 1], F32)
    bwrow = at("bwrow_s", 3392, [1, 48], F32)
    bwrep = at("bwrep", 3584, [128, 48], F32)
    kqm = at("kqm", 3776, [16, 1], F32)
    t16a = at("t16a", 3808, [16, 1], F32)
    t16b = at("t16b", 3840, [16, 1], F32)

    xnT = at("xnT", 6144, [128, 32 * 1024], BF16)
    attnout = at("attnout", 6144, [128, 4 * D], F32)
    hT = at("hT", 6144, [128, 128 * 256], BF16)
    xncast = at("xncast", 71680, [128, D], BF16)
    VT = at("VT", 71680, [128, 32 * 1024], BF16)
    q2T = at("q2T", 71680, [128, 32 * 1024], BF16)
    f_xn3T = at("f_xn3T", 6144, [128, 32 * 1024], BF16)
    f_hT = at("f_hT", 71680, [128, 32 * 1024], BF16)
    f_w1b = [at(f"f_w1b{i}", 137216 + i * 16384, [128, 32 * 256], BF16) for i in range(2)]
    f_w2r = at("f_w2r", 169984, [128, 8 * 512], BF16)
    f_stage = at("f_stage", 178176, [128, 4096], F32)
    f_b2rep = at("f_b2rep", 137216, [128, D], F32)
    f_out = at("f_out", 153600, [128, D], F32)

    xld = [at(f"xld{i}", 137216 + i * 16384, [128, D], F32) for i in range(2)]
    vrow = [at(f"vrow{i}", 137216 + i * 8224, [128, VROW], BF16) for i in range(2)]
    kstage = [at(f"kstage{i}", 137216 + i * 1024, [128, 512], BF16) for i in range(2)]
    ktsq = at("ktsq", 139264, [128, 1024], BF16)
    bdc = [at(f"bdcs{i}", 137216 + i * 2048, [128, 512], F32) for i in range(4)]
    bdh = [at(f"bdh{i}", 145408 + i * 1024, [128, 512], BF16) for i in range(3)]
    bdt1 = at("bdt1", 148480, [128, 512], BF16)
    bdt2 = at("bdt2", 149504, [128, 512], BF16)
    qtraw = [at(f"qtraw{i}", 150528 + i * 4096, [128, 2 * 1024], BF16) for i in range(3)]
    qtsq = at("qtsq", 162816, [128, 1024], BF16)
    ktb = [at(f"ktb{i}", 137216 + i * 8192, [128, 4096], BF16) for i in range(2)]
    vb = [at(f"vbc{i}", 153600 + i * 8224, [128, 16 * 257], BF16) for i in range(2)]

    castb = [at(f"castb{i}", 164864 + i * 4096, [128, 2048], BF16) for i in range(2)]

    expb = [at(f"expb{i}", 204928 + i * 2048, [128, 1024], BF16) for i in range(2)]
    rec4 = at("rec4", 3904, [128, 8], F32)
    xre = at("xre", 172160, [128, D], F32)
    xn3b = at("xn3b", 188544, [128, D], BF16)
    xn3stage = at("xn3stage", 196736, [128, 32 * 128], BF16)

    wbuf = [at(f"wbuf{i}", 176128 + i * 16384, [128, 32 * 256], BF16) for i in range(2)]
    prepc = [at(f"prepc{i}", 176128 + i * 8192, [128, 4096], BF16) for i in range(4)]
    preps = [at(f"preps{i}", 208896 + i * 1024, [128, 512], BF16) for i in range(2)]

    zb = at("zb", 194624, [128, D], F32)

    PA = nc.alloc_psum_tensor("PA", [128, 1024], F32).ap()
    PBt = nc.alloc_psum_tensor("PBt", [128, 1024], F32).ap()
    PC = nc.alloc_psum_tensor("PC", [128, 512], F32).ap()
    PD = nc.alloc_psum_tensor("PD", [128, 512], F32).ap()
    PT = [nc.alloc_psum_tensor(f"PT{i}", [128, 1024], BF16).ap() for i in range(2)]
    P = [PA[:, 0:512], PA[:, 512:1024], PBt[:, 0:512], PBt[:, 512:1024], PC, PD]

    semnames = ["s_sc", "s_x0", "s_x1", "s_wb0", "s_wb1", "s_kt0", "s_kt1",
                "s_vb0", "s_vb1", "s_w10", "s_w11", "s_xq0", "s_xq1",
                "s_w20", "s_w21", "s_gpl", "s_gpw", "s_ps0", "s_ps1",
                "s_cb0", "s_cb1", "s_co0", "s_co1", "s_ks0", "s_ks1",
                "s_vr0", "s_vr1", "s_km", "s_qm", "s_ka", "s_qt",
                "s_zs", "s_st", "s_y", "s_dbg",
                "s_ag1", "s_ag2", "s_ag3",
                "s_pe", "s_ac", "s_dv", "s_po"]
    sems = {n: nc.alloc_semaphore(n) for n in semnames}
    cnt = {n: 0 for n in sems}
    ops = {e: [] for e in ["sync", "scalar", "gpsimd", "tensor", "vector"]}
    ESEM = {"tensor": "s_pe", "scalar": "s_ac", "vector": "s_dv", "gpsimd": "s_po"}

    def op(eng, fn, waits=(), incs=()):
        waits = [(s, v) for (s, v) in waits if v > 0]
        ops[eng].append((fn, waits, list(incs)))
        for s, a in incs:
            cnt[s] += a

    def dma(q, s, out, in_, waits=(), slow=False):
        if slow:
            fn = lambda e, out=out, in_=in_: e.dma_start(out=out, in_=in_, allow_slow_non_contiguous=True)
        else:
            fn = lambda e, out=out, in_=in_: e.dma_start(out=out, in_=in_)
        op(q, fn, waits, [(s, 16)])
        return cnt[s]

    def eop(eng, fn, waits=(), inc=1):
        op(eng, fn, waits, [(ESEM[eng], inc)] if inc else [])
        return cnt[ESEM[eng]]

    def mmop(fn, waits=(), inc=1):
        return eop("tensor", fn, waits, inc)

    def pe_fence():
        return eop("tensor", lambda t: t.nop(), [], inc=1)

    def coll(kind, s, ins, outs, waits=()):
        fn = lambda e, ins=ins, outs=outs: e.collective_compute(
            kind, ALU.bypass, replica_groups=[list(range(NCORES))],
            ins=[ins.opt()], outs=[outs.opt()])
        op("gpsimd", fn, waits, [(s, 1)])
        return cnt[s]

    a1i = ag1_in[0, :]

    dma("scalar", "s_sc", ident[:], ident_i)
    dma("scalar", "s_sc", eh[:], eh_i)
    dma("scalar", "s_sc", bqt[:], bqt_d)
    dma("scalar", "s_sc", bkt[:], bkt_d)
    dma("scalar", "s_sc", bvt[:], bvt_d)
    dma("scalar", "s_sc", b1t[:], b1t_d)
    dma("scalar", "s_sc", ones1f[:], ones1f_i)
    c_bw = dma("scalar", "s_sc", bwrow[:], bw_d)
    c_consts = c_bw
    for _e in ("tensor", "vector", "gpsimd"):
        op(_e, lambda e: e.nop(), [("s_sc", c_consts)], [])
    eop("vector", lambda v: v.memset(epsc[:], EPS))

    prep_cnt = 0
    for mi, wsrc in enumerate([wq_s, wk_s, wv_s]):
        dst = ag1_in[0, mi * MAT_SZ:(mi + 1) * MAT_SZ].rearrange("(p j) -> p j", j=512)
        prep_cnt = dma("gpsimd", "s_gpw", dst, wsrc)
    c_ag1 = coll("AllGather", "s_ag1", ag1_in, ag1_out, waits=[("s_gpw", prep_cnt)])

    if not want("ph1"):
        return _finish(nc, ops, sems, cnt, dbg_outs)
    xn_tr_done = [0] * 8
    for lt in range(8):
        wl = []
        if lt >= 2:
            wl.append(("s_pe", xn_tr_done[lt - 2]))
        c_xl = dma("sync", f"s_x{lt % 2}", xld[lt % 2][:], x_rows[lt * 128:(lt + 1) * 128, :], waits=wl)
        for i in range(8):
            wl2 = [(f"s_x{lt % 2}", c_xl)] if i == 0 else []
            eop("vector", lambda v, i=i, lt=lt: v.bn_stats(
                stats[:, i * 6:(i + 1) * 6], xld[lt % 2][:, i * 512:(i + 1) * 512]), wl2)
        c_st = cnt["s_dv"]
        c_agg = eop("vector", lambda v: v.bn_aggr(aggr[:], stats[:]), [("s_dv", c_st)])
        c_ln = eop("scalar", lambda s: s.activation(t128a[:], aggr[:, 1:2], AF.Ln, bias=epsc[:]),
                   [("s_dv", c_agg)])
        c_rs = eop("scalar", lambda s: s.activation(rstd[:], t128a[:], AF.Exp, scale=-0.5),
                   [("s_ac", c_ln)])
        c_xn = eop("vector", lambda v, lt=lt: v.tensor_scalar(
            xncast[:], xld[lt % 2][:], aggr[:, 0:1], rstd[:], ALU.subtract, ALU.mult),
            [("s_ac", c_rs), ("s_dv", c_agg), ("s_pe", xn_tr_done[lt - 1] if lt >= 1 else 0)])
        for kc in range(32):
            pt = PT[kc % 2]
            waits = [("s_dv", c_xn)] if kc == 0 else []
            if kc >= 2:
                waits.append(("s_dv", cnt["s_dv"]))
            def tr(t, kc=kc, pt=pt):
                return t.transpose(pt[:, 0:128], xncast[:, kc * 128:(kc + 1) * 128], ident[:])
            c_tr = mmop(tr, waits)
            def ev(v, pt=pt, kc=kc, lt=lt):
                return v.tensor_copy(xnT[:, kc * 1024 + lt * 128: kc * 1024 + (lt + 1) * 128],
                                     pt[:, 0:128])
            eop("vector", ev, [("s_pe", c_tr)])
        xn_tr_done[lt] = cnt["s_pe"]
    c_xnT_done = cnt["s_dv"]

    if "xnt" in dbg:
        o = dbg_tensor("xnt", [128, 32 * 1024], BF16)
        dma("gpsimd", "s_dbg", o, xnT[:], waits=[("s_dv", c_xnT_done)])

    def proj(mat_idx, evac_hook, seed_free, extra_first=()):
        pp_free = dict(seed_free)
        wb_free = [cnt["s_pe"], cnt["s_pe"]]
        c_wb = [0, 0]
        rot = 0
        last = {}
        for ocg in range(16):
            b = ocg % 2
            rank = ocg // 2
            col0 = (ocg % 2) * 256
            src = ag1_out[rank, mat_idx * MAT_SZ:(mat_idx + 1) * MAT_SZ] \
                .rearrange("(kc p j) -> p kc j", kc=32, p=128)[:, :, col0:col0 + 256]
            dstv = wbuf[b].rearrange("p (kc j) -> p kc j", kc=32)
            wl = [("s_pe", wb_free[b]), ("s_ag1", c_ag1)]
            if ocg < 2:
                wl.append(("s_dv", c_xnT_done))
                wl += list(extra_first)
            c_wb[b] = dma("sync", f"s_wb{b}", dstv, src, waits=wl)
            for oc2 in range(2):
                oc = ocg * 2 + oc2
                for ncc in range(2):
                    pp = P[2 + rot]
                    for kc in range(32):
                        waits = []
                        if kc == 0:
                            waits = [(f"s_wb{b}", c_wb[b]), pp_free[rot], ("s_dv", c_xnT_done)]
                        def mm(t, pp=pp, b=b, kc=kc, oc2=oc2, ncc=ncc,
                               first=(kc == 0), lastk=(kc == 31)):
                            return t.matmul(pp[:, 0:512],
                                            wbuf[b][:, kc * 256 + oc2 * 128: kc * 256 + oc2 * 128 + 128],
                                            xnT[:, kc * 1024 + ncc * 512: kc * 1024 + ncc * 512 + 512],
                                            start=first, stop=lastk, skip_group_check=True)
                        c_mm = mmop(mm, waits, inc=1 if kc == 31 else 0)
                    pp_free[rot] = evac_hook(oc, ncc, pp, c_mm)
                    last[rot] = pp_free[rot]
                    rot ^= 1
            wb_free[b] = cnt["s_pe"]
        return last

    if not want("vproj"):
        return _finish(nc, ops, sems, cnt, dbg_outs)
    def v_evac(oc, ncc, pp, c_mm):
        def ev(s, oc=oc, ncc=ncc, pp=pp):
            return s.activation(VT[:, oc * 1024 + ncc * 512: oc * 1024 + ncc * 512 + 512],
                                pp[:, 0:512], AF.Identity, bias=bvt[:, oc:oc + 1])
        c = eop("scalar", ev, [("s_pe", c_mm)])
        return ("s_ac", c)
    v_last = proj(2, v_evac, {0: ("s_dv", 0), 1: ("s_dv", 0)})
    c_VT_done = cnt["s_ac"]

    vrow_free = [0, 0]
    vr_tot = [0, 0]
    c_vwr = 0
    ev_hist2 = []
    for j in range(8):
        b = j % 2
        c_ones = eop("vector", lambda v, b=b: v.memset(
            vrow[b].rearrange("p (h c) -> p h c", c=257)[:, :, 256:257], 1.0),
            [(f"s_vr{b}", vrow_free[b])])
        for oc in range(32):
            gidx = j * 32 + oc
            pt = PT[oc % 2]
            waits = []
            if gidx == 0:
                waits.append(("s_ac", c_VT_done))
                waits.append(("s_pe", xn_tr_done[7]))
            if gidx >= 2:
                waits.append(("s_dv", ev_hist2[gidx - 2]))
            def tr(t, oc=oc, j=j, pt=pt):
                return t.transpose(pt[:, 0:128],
                                   VT[:, oc * 1024 + j * 128: oc * 1024 + (j + 1) * 128],
                                   ident[:])
            c_tr = mmop(tr, waits)
            h, sub = oc // 2, oc % 2
            colb = h * 257 + sub * 128
            evw = [("s_pe", c_tr)]
            if oc == 0:
                evw.append(("s_dv", c_ones))
            def ev(v, pt=pt, b=b, colb=colb):
                return v.tensor_copy(vrow[b][:, colb:colb + 128], pt[:, 0:128])
            ev_hist2.append(eop("vector", ev, evw))
        c_as = cnt["s_dv"]
        dst = ag3_in[0, KT_SZ + j * 128 * VROW: KT_SZ + (j + 1) * 128 * VROW] \
            .rearrange("(p c) -> p c", p=128)
        c_vwr = dma("gpsimd", f"s_vr{b}", dst, vrow[b][:], waits=[("s_dv", c_as)])
        vrow_free[b] = c_vwr
        vr_tot[b] = c_vwr

    if not want("kproj"):
        return _finish(nc, ops, sems, cnt, dbg_outs)
    kst_free = [("s_vr0", vr_tot[0]), ("s_vr0", vr_tot[0])]
    kq_started = [False, False]
    kt_view = ag3_in[0, 0:KT_SZ].rearrange("(f m) -> f m", m=ROWS)
    k_state = {"c_kwr": 0}
    def k_evac(oc, ncc, pp, c_mm):
        b = (oc * 2 + ncc) % 2
        def ev(s, pp=pp, b=b, oc=oc):
            return s.activation(kstage[b][:], pp[:, 0:512], AF.Identity, bias=bkt[:, oc:oc + 1])
        c_ev = eop("scalar", ev, [("s_pe", c_mm), kst_free[b]])
        def sq(s, b=b, ncc=ncc):
            return s.activation(ktsq[:, ncc * 512:(ncc + 1) * 512], kstage[b][:], AF.Square)
        c_sq = eop("scalar", sq, [("s_ac", c_ev), ("s_pe", cnt["s_pe"])])
        h = oc // 2
        first = not kq_started[ncc]
        kq_started[ncc] = True
        def mq(t, h=h, ncc=ncc, first=first):
            return t.matmul(P[4 + ncc][0:16, 0:512], eh[:, h * 16:(h + 1) * 16],
                            ktsq[:, ncc * 512:(ncc + 1) * 512], start=first, stop=False,
                            skip_group_check=True)
        mmop(mq, [("s_ac", c_sq)])
        dstk = kt_view[oc * 128:(oc + 1) * 128, ncc * 512:(ncc + 1) * 512]
        k_state["c_kwr"] = dma("gpsimd", f"s_ks{b}", dstk, kstage[b][:], waits=[("s_ac", c_ev)])
        kst_free[b] = (f"s_ks{b}", k_state["c_kwr"])
        k_state[b] = k_state["c_kwr"]
        return ("s_ac", c_ev)
    k_last = proj(1, k_evac, v_last)
    c_kq1 = eop("vector", lambda v: v.tensor_reduce(t16a[:], P[4][0:16, 0:512], AX.X, ALU.max),
                [("s_pe", cnt["s_pe"])])
    c_kq2 = eop("vector", lambda v: v.tensor_reduce(t16b[:], P[5][0:16, 0:512], AX.X, ALU.max),
                [("s_dv", c_kq1)])
    c_kq3 = eop("vector", lambda v: v.tensor_tensor(kqm[:], t16a[:], t16b[:], ALU.max),
                [("s_dv", c_kq2)])
    c_kmax_wr = dma("gpsimd", "s_km", ag3_in[0, KMAX_OFF:KMAX_OFF + 16].rearrange("(k a) -> k a", a=1),
                    kqm[:], waits=[("s_dv", c_kq3)], slow=True)

    c_ag3 = coll("AllGather", "s_ag3", ag3_in, ag3_out,
                 waits=[("s_ks0", k_state.get(0, 0)), ("s_ks1", k_state.get(1, 0)),
                        ("s_km", c_kmax_wr), ("s_vr0", vr_tot[0]), ("s_vr1", vr_tot[1])])

    c_c1 = dma("gpsimd", "s_cb0", ag2_in[0, 0:W1_SZ].rearrange("(p j) -> p j", j=FF), w1_s)
    c_c2 = dma("gpsimd", "s_cb1", ag2_in[0, W1_SZ:].rearrange("(p j) -> p j", j=D), w2_s)
    c_ag2 = coll("AllGather", "s_ag2", ag2_in, ag2_out,
                 waits=[("s_cb0", c_c1), ("s_cb1", c_c2)])
    cast_gate = [("s_cb0", c_c1), ("s_cb1", c_c2)]

    c_bdc = 0
    c_pe_after_k = cnt["s_pe"]
    for i in range(4):
        c_bdc = dma("scalar", "s_sc", bdc[i][:], bd_i[i],
                    waits=[("s_ks0", k_state.get(0, 0)), ("s_ks1", k_state.get(1, 0)),
                           ("s_pe", c_pe_after_k)])
    c_bwmm = mmop(lambda t: t.matmul(P[0][:, 0:48], ones1f[:], bwrow[:], start=True, stop=True,
                                     skip_group_check=True),
                  [("s_sc", c_bw), k_last[0], k_last[1]])
    c_bwrep = eop("vector", lambda v: v.tensor_copy(bwrep[:], P[0][:, 0:48]),
                  [("s_pe", c_bwmm)])

    bd_built = {}
    fold_done_at = {}
    bd_pool_at = {}
    def build_bd(h):
        b = h % 3
        wl = [("s_sc", c_bdc), ("s_dv", c_bwrep)]
        if h == 0:
            wl.append(("s_vr0", vr_tot[0]))
            wl.append(("s_vr1", vr_tot[1]))
        if h >= 1:
            wl.append(("s_po", bd_pool_at[h - 1]))
        if h >= 3:
            wl.append(("s_pe", fold_done_at[h - 3]))
        c1 = eop("vector", lambda v, h=h: v.tensor_scalar(
            bdt1[:], bdc[1][:], bwrep[:, 3 * h:3 * h + 1], None, ALU.mult), wl)
        c2 = eop("vector", lambda v, h=h: v.scalar_tensor_tensor(
            bdt2[:], bdc[2][:], bwrep[:, 3 * h + 1:3 * h + 2], bdc[0][:], ALU.mult, ALU.add), [])
        c3 = eop("gpsimd", lambda g, b=b: g.tensor_tensor(bdh[b][:], bdt1[:], bdt2[:], ALU.add),
                 [("s_dv", c2)])
        bd_pool_at[h] = c3
        c4 = eop("vector", lambda v, h=h, b=b: v.scalar_tensor_tensor(
            bdh[b][:], bdc[3][:], bwrep[:, 3 * h + 2:3 * h + 3], bdh[b][:], ALU.mult, ALU.add),
            [("s_po", c3)])
        bd_built[h] = c4

    build_bd(0)
    build_bd(1)
    qraw_ready = {}
    qq_started = [False, False]
    fold_evac = {0: ("s_dv", c_bwrep), 1: ("s_dv", 0)}

    def emit_fold(h):
        b = h % 3
        for Mc in range(2):
            for n2 in range(2):
                frot = (Mc * 2 + n2) % 2
                pf = P[frot]
                for t2 in range(2):
                    waits = []
                    if t2 == 0:
                        waits = [("s_ac", qraw_ready[h]), ("s_dv", bd_built[h]),
                                 fold_evac[frot]]
                    def mm(t_, h=h, b=b, Mc=Mc, n2=n2, t2=t2, pf=pf):
                        return t_.matmul(pf[:, 0:512],
                                         bdh[b][:, t2 * 256 + Mc * 128: t2 * 256 + Mc * 128 + 128],
                                         qtraw[b][:, t2 * 1024 + n2 * 512: t2 * 1024 + n2 * 512 + 512],
                                         start=(t2 == 0), stop=(t2 == 1), skip_group_check=True)
                    c_f = mmop(mm, waits, inc=1 if t2 == 1 else 0)
                def ev2(v, pf=pf, h=h, Mc=Mc, n2=n2):
                    o2 = (2 * h + Mc) * 1024 + n2 * 512
                    return v.tensor_copy(q2T[:, o2:o2 + 512], pf[:, 0:512])
                c_e2 = eop("vector", ev2, [("s_pe", c_f)])
                fold_evac[frot] = ("s_dv", c_e2)
                def sq(s, h=h, Mc=Mc, n2=n2):
                    o2 = (2 * h + Mc) * 1024 + n2 * 512
                    return s.activation(qtsq[:, n2 * 512:(n2 + 1) * 512],
                                        q2T[:, o2:o2 + 512], AF.Square)
                c_sq = eop("scalar", sq, [("s_dv", c_e2), ("s_pe", cnt["s_pe"])])
                first = not qq_started[n2]
                qq_started[n2] = True
                qw = [("s_ac", c_sq)]
                if first:
                    qw.append(("s_dv", c_kq3))
                def mq(t_, h=h, n2=n2, first=first):
                    return t_.matmul(P[4 + n2][0:16, 0:512], eh[:, h * 16:(h + 1) * 16],
                                     qtsq[:, n2 * 512:(n2 + 1) * 512],
                                     start=first, stop=False, skip_group_check=True)
                mmop(mq, qw)
        fold_done_at[h] = cnt["s_pe"]

    def q_evac(oc, ncc, pp, c_mm):
        h, tt_ = oc // 2, oc % 2
        b = h % 3
        def ev(s, pp=pp, b=b, tt_=tt_, ncc=ncc, oc=oc):
            return s.activation(qtraw[b][:, tt_ * 1024 + ncc * 512: tt_ * 1024 + ncc * 512 + 512],
                                pp[:, 0:512], AF.Identity, bias=bqt[:, oc:oc + 1])
        wl = [("s_pe", c_mm)]
        if h == 0:
            wl.append(("s_vr0", vr_tot[0]))
            wl.append(("s_vr1", vr_tot[1]))
        if h >= 3:
            wl.append(("s_pe", fold_done_at[h - 3]))
        c_ev = eop("scalar", ev, wl)
        if tt_ == 1 and ncc == 1:
            qraw_ready[h] = c_ev
            if h >= 1:
                emit_fold(h - 1)
            if h + 2 < NH:
                build_bd(h + 2)
            if h == NH - 1:
                emit_fold(h)
        return ("s_ac", c_ev)
    proj(0, q_evac, k_last)
    c_q_done = cnt["s_pe"]
    c_qq1 = eop("vector", lambda v: v.tensor_reduce(t16a[:], P[4][0:16, 0:512], AX.X, ALU.max),
                [("s_pe", c_q_done)])
    c_qq2 = eop("vector", lambda v: v.tensor_reduce(t16b[:], P[5][0:16, 0:512], AX.X, ALU.max),
                [("s_dv", c_qq1)])
    c_qq3 = eop("vector", lambda v: v.tensor_tensor(qm2[:], t16a[:], t16b[:], ALU.max),
                [("s_dv", c_qq2)])
    c_qmw = dma("gpsimd", "s_qm", qm_bounce, qm2[:], waits=[("s_dv", c_qq3)], slow=True)

    if "q2t" in dbg:
        o = dbg_tensor("q2t", [128, 32 * 1024], BF16)
        dma("gpsimd", "s_dbg", o, q2T[:], waits=[("s_dv", cnt["s_dv"]), ("s_pe", c_q_done)])
    if "ag3" in dbg:
        o = dbg_tensor("ag3", [1, AG3_N], BF16)
        dma("gpsimd", "s_dbg", o, ag3_in, waits=[("s_ag3", c_ag3)])

    if not want("cbias"):
        return _finish(nc, ops, sems, cnt, dbg_outs)
    c_kall = dma("gpsimd", "s_ka", kall.rearrange("a (r k) -> a r k", r=8),
                 ag3_out[:, KMAX_OFF:KMAX_OFF + 16].unsqueeze(0),
                 waits=[("s_ag3", c_ag3)], slow=True)
    c_qmT = dma("gpsimd", "s_qt", qmT.rearrange("a (k b) -> a k b", b=1), qm_bounce.unsqueeze(0),
                waits=[("s_qm", c_qmw)], slow=True)
    c_km = eop("vector", lambda v: v.tensor_reduce(
        km2[:], kall.rearrange("a (r h) -> a h r", r=8), AX.X, ALU.max),
        [("s_ka", c_kall)])
    c_pr = eop("vector", lambda v: v.tensor_tensor(csc1[:], qmT[:], km2[:], ALU.mult),
               [("s_qt", c_qmT), ("s_dv", c_km)])
    c_cl = eop("scalar", lambda s: s.activation(csc2[:], csc1[:], AF.Ln, bias=epsc[0:1, :]),
               [("s_dv", c_pr)])
    c_ce = eop("scalar", lambda s: s.activation(csc1[:], csc2[:], AF.Exp, scale=0.5),
               [("s_ac", c_cl)])
    c_cbmm = mmop(lambda t: t.matmul(P[2][:, 0:16], ones1f[:], csc1[0:1, :], start=True, stop=True,
                                     skip_group_check=True),
                  [("s_ac", c_ce), fold_evac[0]])
    c_cbias = eop("vector", lambda v: v.tensor_scalar(
        cbias[:], P[2][0:128, 0:16], -1.0, None, ALU.mult), [("s_pe", c_cbmm)])

    if not want("attn"):
        return _finish(nc, ops, sems, cnt, dbg_outs)
    ACC = [PC, PD, PT[0].bitcast(F32), PT[1].bitcast(F32)]
    ktb_free = [0, 0]
    vb_free = [0, 0]
    c_att_done_half = [0, 0]
    c_last_exp = 0
    pacc_gp_gate = []
    for hf in range(2):
        pacc_drained = max(cnt["s_dv"], c_cbias)
        for h in range(NH):
            c_ktl = [0] * 4
            c_vl = [0] * 4
            exp_cnt_at = {}
            av_done_at = {}
            c_st = c_av = 0
            for c4 in range(4):
                b = c4 % 2
                extra = []
                if hf == 0 and h == 0 and c4 < 2:
                    extra = cast_gate + [("s_pe", c_q_done)]
                wl = [("s_pe", ktb_free[b]), ("s_ag3", c_ag3)] + extra
                for ri in range(2):
                    rank = c4 * 2 + ri
                    for ft in range(2):
                        f0 = h * 256 + ft * 128
                        src = ag3_out[rank, f0 * ROWS:(f0 + 128) * ROWS] \
                            .rearrange("(p m) -> p m", p=128)
                        dstk = ktb[b][:, ft * 2048 + ri * 1024: ft * 2048 + ri * 1024 + 1024]
                        c_ktl[c4] = dma("sync", f"s_kt{b}", dstk, src, waits=wl)
                        wl = []
                wl2 = [("s_pe", vb_free[b])] + extra
                for ri in range(2):
                    rank = c4 * 2 + ri
                    src = ag3_out[rank, KT_SZ:KT_SZ + V_SZ] \
                        .rearrange("(mt p c) -> p mt c", mt=8, p=128)[:, :, h * 257:(h + 1) * 257]
                    dstv = vb[b].rearrange("p (mt c) -> p mt c", c=257)[:, ri * 8:(ri + 1) * 8, :]
                    c_vl[c4] = dma("sync", f"s_vb{b}", dstv, src, waits=wl2)
                    wl2 = []
                for prc in range(8):
                    pr = c4 * 8 + prc
                    pst = [PA, PBt][pr % 2]
                    waits = []
                    if pr == 0:
                        waits = [(f"s_kt{b}", c_ktl[c4]), ("s_dv", pacc_drained)]
                        if hf == 1:
                            waits += pacc_gp_gate
                    elif prc == 0:
                        waits = [(f"s_kt{b}", c_ktl[c4])]
                    if pr >= 2:
                        waits.append(("s_ac", exp_cnt_at[pr - 2]))
                    for gi in range(2):
                        mt = prc * 2 + gi
                        for ft in range(2):
                            def mm(t, b=b, ft=ft, mt=mt, h=h, hf=hf, pst=pst, gi=gi):
                                return t.matmul(pst[:, gi * 512:(gi + 1) * 512],
                                                ktb[b][:, ft * 2048 + mt * 128: ft * 2048 + mt * 128 + 128],
                                                q2T[:, (2 * h + ft) * 1024 + hf * 512:
                                                    (2 * h + ft) * 1024 + hf * 512 + 512],
                                                start=(ft == 0), stop=(ft == 1), skip_group_check=True)
                            c_st = mmop(mm, waits if (gi == 0 and ft == 0) else [],
                                        inc=1 if (gi == 1 and ft == 1) else 0)
                    ew = [("s_pe", c_st)]
                    if pr >= 2:
                        ew.append(("s_pe", av_done_at[pr - 2]))
                    def ex(s, pr=pr, h=h, pst=pst):
                        return s.activation(expb[pr % 2][:], pst[:, 0:1024], AF.Exp,
                                            bias=cbias[:, h:h + 1])
                    exp_cnt_at[pr] = eop("scalar", ex, ew)
                    avw = [("s_ac", exp_cnt_at[pr]), (f"s_vb{b}", c_vl[c4])]
                    for gi in range(2):
                        mt = prc * 2 + gi
                        g = c4 * 16 + mt
                        for lc in range(4):
                            def mm2(t, g=g, b=b, mt=mt, lc=lc, gi=gi, pr=pr,
                                    first=(g == 0), lastg=(g == 63)):
                                return t.matmul(ACC[lc][:, 0:257],
                                                expb[pr % 2][:, gi * 512 + lc * 128: gi * 512 + lc * 128 + 128],
                                                vb[b][:, mt * 257:(mt + 1) * 257],
                                                start=first, stop=lastg, skip_group_check=True)
                            c_av = mmop(mm2, avw if (gi == 0 and lc == 0) else [],
                                        inc=1 if (gi == 1 and lc == 3) else 0)
                    av_done_at[pr] = c_av
                ktb_free[b] = c_st
                vb_free[b] = c_av
            c_last_exp = exp_cnt_at[31]
            c_rc = 0
            for lc in range(4):
                c_rc = eop("vector", lambda v, lc=lc: v.reciprocal(
                    rec4[:, lc:lc + 1], ACC[lc][:, 256:257]), [("s_pe", av_done_at[31])])
            for lc in range(4):
                def sc(v, lc=lc, h=h):
                    return v.tensor_scalar(attnout[:, lc * D + h * 256: lc * D + h * 256 + 256],
                                           ACC[lc][:, 0:256], rec4[:, lc:lc + 1], None, ALU.mult)
                eop("vector", sc, [("s_dv", c_rc)])
            pacc_drained = cnt["s_dv"]
        c_att_done_half[hf] = cnt["s_dv"]

        if "att" in dbg and hf == 0:
            o = dbg_tensor("att", [128, 4 * D], F32)
            dma("gpsimd", "s_dbg", o, attnout[:], waits=[("s_dv", c_att_done_half[0])])

        c_z_prev = 0
        c_stagedma_prev = 0
        xn3_tr_prev = 0
        for lt4 in range(4):
            lt = hf * 4 + lt4
            aosl = attnout[:, lt4 * D:(lt4 + 1) * D]
            c_xr = dma("scalar", "s_sc", xre[:], x_rows[lt * 128:(lt + 1) * 128, :],
                       waits=[("s_dv", c_z_prev if lt4 > 0 else c_att_done_half[hf])])
            for i in range(8):
                wl3 = [("s_dv", c_att_done_half[hf])] if i == 0 else []
                eop("vector", lambda v, i=i, aosl=aosl: v.bn_stats(
                    stats[:, i * 6:(i + 1) * 6], aosl[:, i * 512:(i + 1) * 512]), wl3)
            c_st2 = cnt["s_dv"]
            c_ag2v = eop("vector", lambda v: v.bn_aggr(aggr[:], stats[:]), [("s_dv", c_st2)])
            c_ln2 = eop("scalar", lambda s: s.activation(t128a[:], aggr[:, 1:2], AF.Ln, bias=epsc[:]),
                        [("s_dv", c_ag2v)])
            c_rs2 = eop("scalar", lambda s: s.activation(rstd[:], t128a[:], AF.Exp, scale=-0.5),
                        [("s_ac", c_ln2)])
            c_mr = eop("vector", lambda v: v.tensor_tensor(t128b[:], aggr[:, 0:1], rstd[:], ALU.mult),
                       [("s_ac", c_rs2), ("s_dv", c_ag2v)])
            c_x2 = eop("vector", lambda v: v.tensor_scalar(
                xre[:], xre[:], t128b[:], None, ALU.subtract),
                [("s_dv", c_mr), ("s_sc", c_xr)])
            c_z = eop("vector", lambda v, aosl=aosl: v.scalar_tensor_tensor(
                aosl[:], aosl[:], rstd[:], xre[:], ALU.mult, ALU.add),
                [("s_dv", c_x2)])
            c_z_prev = c_z
            c_zs = dma("gpsimd", "s_zs", z_spill[lt * 128:(lt + 1) * 128, :], aosl[:],
                       waits=[("s_dv", c_z)])
            for i in range(8):
                eop("vector", lambda v, i=i, aosl=aosl: v.bn_stats(
                    stats[:, i * 6:(i + 1) * 6], aosl[:, i * 512:(i + 1) * 512]),
                    [("s_dv", c_z)] if i == 0 else [])
            c_st3 = cnt["s_dv"]
            c_ag3v = eop("vector", lambda v: v.bn_aggr(aggr[:], stats[:]), [("s_dv", c_st3)])
            c_ln3 = eop("scalar", lambda s: s.activation(t128a[:], aggr[:, 1:2], AF.Ln, bias=epsc[:]),
                        [("s_dv", c_ag3v)])
            c_rs3 = eop("scalar", lambda s: s.activation(rstd[:], t128a[:], AF.Exp, scale=-0.5),
                        [("s_ac", c_ln3)])
            c_x3 = eop("vector", lambda v, aosl=aosl: v.tensor_scalar(
                xn3b[:], aosl[:], aggr[:, 0:1], rstd[:], ALU.subtract, ALU.mult),
                [("s_ac", c_rs3), ("s_dv", c_ag3v), ("s_pe", xn3_tr_prev)])
            for kc in range(32):
                pt = PT[kc % 2]
                waits = [("s_dv", c_x3)] if kc == 0 else []
                if kc == 0:
                    waits.append(("s_st", c_stagedma_prev))
                if kc >= 2:
                    waits.append(("s_dv", cnt["s_dv"]))
                def tr(t, kc=kc, pt=pt):
                    return t.transpose(pt[:, 0:128], xn3b[:, kc * 128:(kc + 1) * 128], ident[:])
                c_tr = mmop(tr, waits)
                def ev(v, pt=pt, kc=kc):
                    return v.tensor_copy(xn3stage[:, kc * 128:(kc + 1) * 128], pt[:, 0:128])
                eop("vector", ev, [("s_pe", c_tr)])
            xn3_tr_prev = cnt["s_pe"]
            c_sta = cnt["s_dv"]
            dst3 = x3t_spill.rearrange("(ft p) m -> p ft m", p=128)[:, :, lt * 128:(lt + 1) * 128]
            srcv = xn3stage.rearrange("p (ft l) -> p ft l", ft=32)
            c_stagedma_prev = dma("gpsimd", "s_st", dst3, srcv, waits=[("s_dv", c_sta)])
        pacc_gp_gate = [("s_st", cnt["s_st"]), ("s_zs", cnt["s_zs"])]
    c_3b_done = None
    c_3b_dv = cnt["s_dv"]
    gate_3b = [("s_st", cnt["s_st"]), ("s_zs", cnt["s_zs"])]
    if "z" in dbg:
        o = dbg_tensor("z", [ROWS, D], F32)
        dma("gpsimd", "s_dbg", o, z_spill, waits=gate_3b)

    if not want("ffn"):
        return _finish(nc, ops, sems, cnt, dbg_outs)
    FCH = 4
    PB = [PA[:, 0:512], PA[:, 512:1024], PBt[:, 0:512], PBt[:, 512:1024],
          PC[:, 0:512], PD[:, 0:512],
          PT[0].bitcast(F32)[:, 0:512], PT[1].bitcast(F32)[:, 0:512]]

    srcx = x3t_spill.rearrange("(ft p) m -> p ft m", p=128)
    dstx = f_xn3T.rearrange("p (ft l) -> p ft l", ft=32)
    c_x3f = dma("sync", "s_xq0", dstx, srcx, waits=[("s_pe", cnt["s_pe"])] + gate_3b)

    w1b_free = [0, 0]
    w2_free = {0: 0, 1: 0}
    pb_evac = {0: ("s_ac", c_last_exp), 1: ("s_ac", c_last_exp),
               2: ("s_dv", c_att_done_half[1]), 3: ("s_dv", c_att_done_half[1]),
               4: ("s_dv", c_att_done_half[1]), 5: ("s_dv", c_att_done_half[1]),
               6: ("s_dv", c_3b_dv), 7: ("s_dv", c_3b_dv)}
    gelu_at = {}
    c_os = 0
    OS = "s_gpl"
    os_total_prev = 0
    for Fch in range(FCH):
        for fg in range(16):
            bw_ = fg % 2
            wlw1 = [("s_pe", w1b_free[bw_]), ("s_ag2", c_ag2)]
            c_w1 = 0
            for rk in range(8):
                src1 = ag2_out[rk, 0:W1_SZ].rearrange("(kcl p f) -> p kcl f", kcl=4, p=128)[
                    :, :, Fch * 4096 + fg * 256: Fch * 4096 + (fg + 1) * 256]
                dst1 = f_w1b[bw_].rearrange("p (kc j) -> p kc j", kc=32)[:, rk * 4:(rk + 1) * 4, :]
                c_w1 = dma("sync", f"s_w1{bw_}", dst1, src1, waits=wlw1)
                wlw1 = []
            for f2 in range(2):
                fc = fg * 2 + f2
                gfc = Fch * 32 + fc
                for nc2 in range(2):
                    rot = (fc * 2 + nc2) % 2
                    pp = P[rot]
                    for kc in range(32):
                        waits = []
                        if kc == 0:
                            waits = [(f"s_w1{bw_}", c_w1), (f"s_xq0", c_x3f), pb_evac[rot]]
                            if (fc * 2 + nc2) >= 2:
                                waits.append(("s_ac", gelu_at[(Fch, fc * 2 + nc2 - 2)]))
                        def mm(t, pp=pp, bw_=bw_, kc=kc, f2=f2, nc2=nc2):
                            return t.matmul(pp[:, 0:512],
                                            f_w1b[bw_][:, kc * 256 + f2 * 128: kc * 256 + f2 * 128 + 128],
                                            f_xn3T[:, kc * 1024 + nc2 * 512: kc * 1024 + nc2 * 512 + 512],
                                            start=(kc == 0), stop=(kc == 31), skip_group_check=True)
                        c_m1 = mmop(mm, waits, inc=1 if kc == 31 else 0)
                    def ge(s, pp=pp, fc=fc, nc2=nc2, gfc=gfc):
                        return s.activation(f_hT[:, fc * 1024 + nc2 * 512: fc * 1024 + nc2 * 512 + 512],
                                            pp[:, 0:512], AF.Gelu_apprx_tanh,
                                            bias=b1t[:, gfc:gfc + 1])
                    gelu_at[(Fch, fc * 2 + nc2)] = eop("scalar", ge, [("s_pe", c_m1)])
            w1b_free[bw_] = c_m1
        c_gelu_all = gelu_at[(Fch, 63)]
        for ncd in range(8):
            w2_free_pe = cnt["s_pe"]
            c_m2 = 0
            for kcg in range(8):
                kc0 = kcg * 4
                frow = Fch * 4096 + kc0 * 128
                rank = frow // 2048
                rloc = frow % 2048
                slot = kcg % 2
                src2 = ag2_out[rank, W1_SZ + rloc * D: W1_SZ + (rloc + 512) * D] \
                    .rearrange("(kk p d) -> p kk d", kk=4, p=128)[:, :, ncd * 512:(ncd + 1) * 512]
                dst2 = f_w2r.rearrange("p (t j) -> p t j", t=8)[:, slot * 4:(slot + 1) * 4, :]
                wl4 = [("s_pe", w2_free[slot]), ("s_ag2", c_ag2)]
                c_w2l = dma("sync", f"s_w2{slot}", dst2, src2, waits=wl4)
                for kk in range(4):
                    kc = kc0 + kk
                    for Mc in range(8):
                        waits = []
                        if kk == 0 and Mc == 0:
                            waits = [(f"s_w2{slot}", c_w2l)]
                        if kc == 0:
                            if ncd == 0 and Mc == 0:
                                waits.append(("s_ac", c_gelu_all))
                            waits.append(pb_evac[Mc])
                        def mm(t, kc=kc, Mc=Mc, slot=slot, kk=kk):
                            return t.matmul(PB[Mc],
                                            f_hT[:, kc * 1024 + Mc * 128: kc * 1024 + Mc * 128 + 128],
                                            f_w2r[:, (slot * 4 + kk) * 512:(slot * 4 + kk) * 512 + 512],
                                            start=(kc == 0), stop=(kc == 31), skip_group_check=True)
                        c_m2 = mmop(mm, waits, inc=1 if (kk == 3 and Mc == 7) else 0)
                w2_free[slot] = c_m2
            for Mc in range(8):
                def ev(v, Mc=Mc):
                    return v.tensor_copy(f_stage[:, Mc * 512:(Mc + 1) * 512], PB[Mc])
                wl5 = [("s_pe", c_m2), (OS, c_os)]
                pb_evac[Mc] = ("s_dv", eop("vector", ev, wl5))
            c_ev8 = cnt["s_dv"]
            srcs = f_stage.rearrange("p (Mc d) -> p Mc d", Mc=8)
            dsts = out_spill.rearrange("(Mc p) d -> p Mc d", p=128)[:, :, ncd * 512:(ncd + 1) * 512]
            wl6 = [("s_dv", c_ev8)]
            if ncd == 0:
                wl6.append((OS, os_total_prev))
            if Fch == 0:
                c_os = dma("gpsimd", OS, dsts, srcs, waits=wl6)
            else:
                def fnadd(e, dsts=dsts, srcs=srcs):
                    return e.dma_start(out=dsts, in_=srcs, accum_op=ALU.add)
                op("gpsimd", fnadd, wl6, [(OS, 16)])
                c_os = cnt[OS]
        os_total_prev = c_os

    c_b2l = dma("scalar", "s_sc", f_out[0:1, :], b2_d,
                waits=[("s_pe", w1b_free[0]), ("s_pe", w1b_free[1])])
    b2ev = {}
    c_b2mm = 0
    for i in range(8):
        wlb = [("s_sc", c_b2l), pb_evac[i % 2]]
        if i >= 2:
            wlb.append(("s_dv", b2ev[i - 2]))
        def mmb(t, i=i):
            return t.matmul(P[i % 2][:, 0:512], ones1f[:], f_out[0:1, i * 512:(i + 1) * 512],
                            start=True, stop=True, skip_group_check=True)
        c_b2mm = mmop(mmb, wlb)
        def evb(v, i=i):
            return v.tensor_copy(f_b2rep[:, i * 512:(i + 1) * 512], P[i % 2][:, 0:512])
        b2ev[i] = eop("vector", evb, [("s_pe", c_b2mm)])
        pb_evac[i % 2] = ("s_dv", b2ev[i])
    c_b2rep = cnt["s_dv"]

    zb_gate = ("s_pe", c_b2mm)
    for lt in range(8):
        c_ol = dma("scalar", "s_sc", f_out[:], out_spill[lt * 128:(lt + 1) * 128, :],
                   waits=[(OS, c_os), ("s_pe", c_b2mm),
                          ("s_dv", cnt["s_dv"] if lt > 0 else c_b2rep)])
        c_ob = eop("vector", lambda v: v.tensor_tensor(f_out[:], f_out[:], f_b2rep[:], ALU.add),
                   [("s_sc", c_ol), ("s_dv", c_b2rep)])
        for i in range(8):
            eop("vector", lambda v, i=i: v.bn_stats(
                stats[:, i * 6:(i + 1) * 6], f_out[:, i * 512:(i + 1) * 512]),
                [("s_dv", c_ob)] if i == 0 else [])
        c_st4 = cnt["s_dv"]
        c_ag4 = eop("vector", lambda v: v.bn_aggr(aggr[:], stats[:]), [("s_dv", c_st4)])
        c_ln4 = eop("scalar", lambda s: s.activation(t128a[:], aggr[:, 1:2], AF.Ln, bias=epsc[:]),
                    [("s_dv", c_ag4)])
        c_rs4 = eop("scalar", lambda s: s.activation(rstd[:], t128a[:], AF.Exp, scale=-0.5),
                    [("s_ac", c_ln4)])
        c_zl = dma("scalar", "s_sc", zb[:], z_spill[lt * 128:(lt + 1) * 128, :],
                   waits=[zb_gate, ("s_ac", c_rs4)])
        c_mr4 = eop("vector", lambda v: v.tensor_tensor(t128b[:], aggr[:, 0:1], rstd[:], ALU.mult),
                    [("s_ac", c_rs4), ("s_dv", c_ag4)])
        c_z4 = eop("vector", lambda v: v.tensor_scalar(
            zb[:], zb[:], t128b[:], None, ALU.subtract),
            [("s_dv", c_mr4), ("s_sc", c_zl)])
        c_fin = eop("vector", lambda v: v.scalar_tensor_tensor(
            zb[:], f_out[:], rstd[:], zb[:], ALU.mult, ALU.add),
            [("s_dv", c_z4)])
        c_y = dma("scalar", "s_y", y_rows[lt * 128:(lt + 1) * 128, :], zb[:],
                  waits=[("s_dv", c_fin)])
        zb_gate = ("s_y", c_y)
    if "ht" in dbg:
        o = dbg_tensor("ht", [128, 128 * 256], BF16)
        dma("gpsimd", "s_dbg", o, hT[:], waits=[("s_ac", cnt["s_ac"])])

    return _finish(nc, ops, sems, cnt, dbg_outs)


def _finish(nc, ops, sems, cnt, dbg_outs):
    ops["gpsimd"].append((lambda g: g.nop(),
       [(s, v) for s, v in cnt.items() if v > 0], []))
    with nc.Block() as blk:
        def emit(e, eng):
            for fn, waits, incs in ops[eng]:
                for s, v in waits:
                    e.wait_ge(sems[s], v)
                inst = fn(e)
                for s, a in incs:
                    inst = inst.then_inc(sems[s], a)
        blk.sync(lambda e: emit(e, "sync"))
        blk.scalar(lambda e: emit(e, "scalar"))
        blk.gpsimd(lambda e: emit(e, "gpsimd"))
        blk.tensor(lambda e: emit(e, "tensor"))
        blk.vector(lambda e: emit(e, "vector"))
    return nc, dbg_outs


def _host_inputs(inp):
    x = np.ascontiguousarray(np.asarray(inp["L0_attn_x"], dtype=np.float32).reshape(L, D))
    wq = np.asarray(inp["L0_attn_Wq"], dtype=np.float32)
    wk = np.asarray(inp["L0_attn_Wk"], dtype=np.float32)
    wv = np.asarray(inp["L0_attn_Wv"], dtype=np.float32)
    w1 = np.asarray(inp["L0_ffn_inter_w"], dtype=np.float32)
    w2 = np.asarray(inp["L0_ffn_output_w"], dtype=np.float32)
    bq = np.asarray(inp["L0_attn_bq"], dtype=np.float32)
    bk = np.asarray(inp["L0_attn_bk"], dtype=np.float32)
    bv = np.asarray(inp["L0_attn_bv"], dtype=np.float32)
    b1 = np.asarray(inp["L0_ffn_inter_b"], dtype=np.float32)
    b2 = np.asarray(inp["L0_ffn_output_b"], dtype=np.float32)
    bw = np.asarray(inp["L0_attn_bias_weight"], dtype=np.float32)
    bqt = np.ascontiguousarray(bq.reshape(32, 128).T)
    bkt = np.ascontiguousarray(bk.reshape(32, 128).T)
    bvt = np.ascontiguousarray(bv.reshape(32, 128).T)
    b1t = np.ascontiguousarray(b1.reshape(128, 128).T)
    maps = []
    for c in range(NCORES):
        maps.append({
            "x_rows": np.ascontiguousarray(x[c * ROWS:(c + 1) * ROWS]),
            "wq_s": np.ascontiguousarray(wq[c * 512:(c + 1) * 512].T),
            "wk_s": np.ascontiguousarray(wk[c * 512:(c + 1) * 512].T),
            "wv_s": np.ascontiguousarray(wv[c * 512:(c + 1) * 512].T),
            "w1_s": np.ascontiguousarray(w1[c * 512:(c + 1) * 512]),
            "w2_s": np.ascontiguousarray(w2[c * 2048:(c + 1) * 2048]),
            "bqt": bqt, "bkt": bkt, "bvt": bvt, "b1t": b1t,
            "b2row": np.ascontiguousarray(b2.reshape(1, D)),
            "bwrow": np.ascontiguousarray(bw.reshape(1, 48)),
        })
    return maps


def kernel(**inputs):
    nc, _ = build_nc()
    in_maps = _host_inputs(inputs)
    res = run_bass_kernel_spmd(nc, in_maps, core_ids=list(range(NCORES)))
    out = np.concatenate([r["y_rows"] for r in res.results], axis=0)
    return out.reshape(1, L, D)


# version 04
# speedup vs baseline: 1.1717x; 1.1717x over previous
import numpy as np
import ml_dtypes
import concourse.bass as bass
import concourse.mybir as mybir
from concourse.bass_utils import run_bass_kernel_spmd

F32 = mybir.dt.float32
BF16 = mybir.dt.bfloat16
AF = mybir.ActivationFunctionType
ALU = mybir.AluOpType
AX = mybir.AxisListType

NCORES = 8
L = 8192
D = 4096
ROWS = L // NCORES
NH = 16
FF = 16384
CD = 32
EPS = 1e-5
SCALE = 1.0 / 16.0

def _build_tables():
    g = [1.0, 1.0, 1.0]
    dim, n = 3, 8
    masks = sorted(range(n), key=lambda m: (bin(m).count('1'), m))
    inv = {m: i for i, m in enumerate(masks)}
    C = np.zeros((n, n, n), dtype=np.float64)
    for i, a in enumerate(masks):
        for j, b in enumerate(masks):
            sw, at = 0, a >> 1
            while at:
                sw += bin(at & b).count('1')
                at >>= 1
            s = -1.0 if sw & 1 else 1.0
            for t in range(dim):
                if a >> t & 1 and b >> t & 1:
                    s *= g[t]
            C[i, j, inv[a ^ b]] = s
    grades = np.array([bin(m).count('1') for m in masks])
    rev = np.where(grades * (grades - 1) // 2 % 2 == 1, -1.0, 1.0)
    return C, rev

_C8, _REV = _build_tables()
_BIV = [4, 5, 6]

def _bd_consts():
    I32 = np.eye(CD)
    out = [np.kron(I32, SCALE * np.diag(_REV))]
    for c in range(3):
        out.append(np.kron(I32, _REV[:, None] * _C8[:, :, _BIV[c]]))
    return [b.astype(np.float32) for b in out]

W_BLK = 128 * 512
MAT_SZ = 32 * W_BLK
AG1_N = 3 * MAT_SZ
W1_SZ = 512 * FF
W2_SZ = 2048 * D
AG2_N = W1_SZ + W2_SZ
KT_SZ = D * ROWS
VROW = NH * 257
V_SZ = ROWS * VROW
KMAX_OFF = KT_SZ
AG3_N = KMAX_OFF + 16


PHASES = ["consts", "prep", "ph1", "vproj", "kproj", "qproj", "cbias", "attn", "ffn"]

def build_nc(dbg=(), stop=None):
    dbg = set(dbg)
    _stop_i = PHASES.index(stop) if stop else len(PHASES) - 1
    def want(name):
        return PHASES.index(name) <= _stop_i
    nc = bass.Bass("TRN2", target_bir_lowering=False)

    x_rows = nc.dram_tensor("x_rows", [ROWS, D], F32, kind="ExternalInput").ap()
    wq_s = nc.dram_tensor("wq_s", [D, 512], F32, kind="ExternalInput").ap()
    wk_s = nc.dram_tensor("wk_s", [D, 512], F32, kind="ExternalInput").ap()
    wv_s = nc.dram_tensor("wv_s", [D, 512], F32, kind="ExternalInput").ap()
    w1_s = nc.dram_tensor("w1_s", [512, FF], F32, kind="ExternalInput").ap()
    w2_s = nc.dram_tensor("w2_s", [2048, D], F32, kind="ExternalInput").ap()
    bqt_d = nc.dram_tensor("bqt", [128, 32], F32, kind="ExternalInput").ap()
    bkt_d = nc.dram_tensor("bkt", [128, 32], F32, kind="ExternalInput").ap()
    bvt_d = nc.dram_tensor("bvt", [128, 32], F32, kind="ExternalInput").ap()
    b1t_d = nc.dram_tensor("b1t", [128, 128], F32, kind="ExternalInput").ap()
    b2_d = nc.dram_tensor("b2row", [1, D], F32, kind="ExternalInput").ap()
    bw_d = nc.dram_tensor("bwrow", [1, 48], F32, kind="ExternalInput").ap()
    y_rows = nc.dram_tensor("y_rows", [ROWS, D], F32, kind="ExternalOutput").ap()

    ag1_in = nc.dram_tensor("ag1_in", [1, AG1_N], BF16, kind="Internal").ap()
    ag1_out = nc.dram_tensor("ag1_out", [NCORES, AG1_N], BF16, kind="Internal", addr_space="Shared").ap()
    ag2_in = nc.dram_tensor("ag2_in", [1, AG2_N], BF16, kind="Internal").ap()
    ag2_out = nc.dram_tensor("ag2_out", [NCORES, AG2_N], BF16, kind="Internal", addr_space="Shared").ap()
    ag3_in = nc.dram_tensor("ag3_in", [1, AG3_N], BF16, kind="Internal").ap()
    ag3_out = nc.dram_tensor("ag3_out", [NCORES, AG3_N], BF16, kind="Internal", addr_space="Shared").ap()
    agv_in = nc.dram_tensor("agv_in", [1, V_SZ], BF16, kind="Internal").ap()
    agv_out = nc.dram_tensor("agv_out", [NCORES, V_SZ], BF16, kind="Internal", addr_space="Shared").ap()
    z_spill = nc.dram_tensor("z_spill", [ROWS, D], F32, kind="Internal").ap()
    x3t_spill = nc.dram_tensor("x3t_spill", [D, ROWS], BF16, kind="Internal").ap()
    qm_bounce = nc.dram_tensor("qm_bounce", [16, 1], F32, kind="Internal").ap()
    out_spill = nc.dram_tensor("out_spill", [ROWS, D], F32, kind="Internal").ap()

    dbg_outs = {}
    def dbg_tensor(name, shape, dt):
        dbg_outs[name] = nc.dram_tensor("o_" + name, shape, dt, kind="ExternalOutput").ap()
        return dbg_outs[name]

    ident_i = nc.inline_tensor(np.eye(128, dtype=ml_dtypes.bfloat16), name="identc").ap()
    eh_np = np.zeros((128, 256), dtype=ml_dtypes.bfloat16)
    for h in range(16):
        eh_np[:, h * 16 + h] = 1.0
    eh_i = nc.inline_tensor(np.ascontiguousarray(eh_np), name="ehc").ap()
    ones1f_i = nc.inline_tensor(np.ones((1, 128), dtype=np.float32), name="ones1fc").ap()
    bd_i = []
    for i, b in enumerate(_bd_consts()):
        m = np.concatenate([b[:128, :], b[128:, :]], axis=1).astype(np.float32)
        bd_i.append(nc.inline_tensor(np.ascontiguousarray(m), name=f"bdc{i}").ap())

    base = (nc.sbuf_base + 31) & ~31
    def at(name, off, shape, dt):
        return nc.alloc_sbuf_tensor_at(name, shape, dt, offset=base + off).ap()

    ident = at("ident", 0, [128, 128], BF16)
    eh = at("eh", 256, [128, 256], BF16)
    bqt = at("bqt_s", 768, [128, 32], F32)
    bkt = at("bkt_s", 896, [128, 32], F32)
    bvt = at("bvt_s", 1024, [128, 32], F32)
    b1t = at("b1t_s", 1152, [128, 128], F32)
    ones1f = at("ones1f", 1664, [1, 128], F32)
    cbias = at("cbias", 2176, [128, 16], F32)
    kall = at("kall", 2240, [1, 128], F32)
    km2 = at("km2", 2752, [1, 16], F32)
    qm2 = at("qm2", 2816, [16, 1], F32)
    qmT = at("qmT", 2848, [1, 16], F32)
    csc1 = at("csc1", 2912, [1, 16], F32)
    csc2 = at("csc2", 2976, [1, 16], F32)
    stats = at("stats", 3040, [128, 48], F32)
    aggr = at("aggr", 3232, [128, 2], F32)
    rstd = at("rstd", 3264, [128, 1], F32)
    t128a = at("t128a", 3296, [128, 1], F32)
    t128b = at("t128b", 3328, [128, 1], F32)
    epsc = at("epsc", 3360, [128, 1], F32)
    bwrow = at("bwrow_s", 3392, [1, 48], F32)
    bwrep = at("bwrep", 3584, [128, 48], F32)
    kqm = at("kqm", 3776, [16, 1], F32)
    t16a = at("t16a", 3808, [16, 1], F32)
    t16b = at("t16b", 3840, [16, 1], F32)

    xnT = at("xnT", 6144, [128, 32 * 1024], BF16)
    attnout = at("attnout", 6144, [128, 4 * D], F32)
    hT = at("hT", 6144, [128, 128 * 256], BF16)
    xncast = at("xncast", 71680, [128, D], BF16)
    VT = at("VT", 71680, [128, 32 * 1024], BF16)
    q2T = at("q2T", 71680, [128, 32 * 1024], BF16)
    f_xn3T = at("f_xn3T", 6144, [128, 32 * 1024], BF16)
    f_hT = at("f_hT", 71680, [128, 32 * 1024], BF16)
    f_w1b = [at(f"f_w1b{i}", 137216 + i * 16384, [128, 32 * 256], BF16) for i in range(2)]
    f_w2r = at("f_w2r", 169984, [128, 8 * 512], BF16)
    f_stage = at("f_stage", 178176, [128, 4096], F32)
    f_b2rep = at("f_b2rep", 137216, [128, D], F32)
    f_out = at("f_out", 153600, [128, D], F32)

    xld = [at(f"xld{i}", 137216 + i * 16384, [128, D], F32) for i in range(2)]
    vrow = [at(f"vrow{i}", 137216 + i * 8224, [128, VROW], BF16) for i in range(2)]
    kstage = [at(f"kstage{i}", 137216 + i * 1024, [128, 512], BF16) for i in range(2)]
    ktsq = at("ktsq", 139264, [128, 1024], BF16)
    bdc = [at(f"bdcs{i}", 137216 + i * 2048, [128, 512], F32) for i in range(4)]
    bdh = [at(f"bdh{i}", 145408 + i * 1024, [128, 512], BF16) for i in range(3)]
    bdt1 = at("bdt1", 148480, [128, 512], BF16)
    bdt2 = at("bdt2", 149504, [128, 512], BF16)
    qtraw = [at(f"qtraw{i}", 150528 + i * 4096, [128, 2 * 1024], BF16) for i in range(3)]
    qtsq = at("qtsq", 162816, [128, 1024], BF16)
    ktb = [at(f"ktb{i}", 137216 + i * 8192, [128, 4096], BF16) for i in range(2)]
    vb = [at(f"vbc{i}", 153600 + i * 8224, [128, 16 * 257], BF16) for i in range(2)]

    castb = [at(f"castb{i}", 164864 + i * 4096, [128, 2048], BF16) for i in range(2)]

    expb = [at(f"expb{i}", 204928 + i * 2048, [128, 1024], BF16) for i in range(2)]
    rec4 = at("rec4", 3904, [128, 8], F32)
    xre = at("xre", 172160, [128, D], F32)
    xn3b = at("xn3b", 188544, [128, D], BF16)
    xn3stage = at("xn3stage", 196736, [128, 32 * 128], BF16)

    wbuf = [at(f"wbuf{i}", 176128 + i * 16384, [128, 32 * 256], BF16) for i in range(2)]
    prepc = [at(f"prepc{i}", 176128 + i * 8192, [128, 4096], BF16) for i in range(4)]
    preps = [at(f"preps{i}", 208896 + i * 1024, [128, 512], BF16) for i in range(2)]

    zb = at("zb", 194624, [128, D], F32)

    PA = nc.alloc_psum_tensor("PA", [128, 1024], F32).ap()
    PBt = nc.alloc_psum_tensor("PBt", [128, 1024], F32).ap()
    PC = nc.alloc_psum_tensor("PC", [128, 512], F32).ap()
    PD = nc.alloc_psum_tensor("PD", [128, 512], F32).ap()
    PT = [nc.alloc_psum_tensor(f"PT{i}", [128, 1024], BF16).ap() for i in range(2)]
    P = [PA[:, 0:512], PA[:, 512:1024], PBt[:, 0:512], PBt[:, 512:1024], PC, PD]

    semnames = ["s_sc", "s_x0", "s_x1", "s_wb0", "s_wb1", "s_kt0", "s_kt1",
                "s_vb0", "s_vb1", "s_w10", "s_w11", "s_xq0", "s_xq1",
                "s_w20", "s_w21", "s_gpl", "s_gpw", "s_ps0", "s_ps1",
                "s_cb0", "s_cb1", "s_co0", "s_co1", "s_ks0", "s_ks1",
                "s_vr0", "s_vr1", "s_km", "s_qm", "s_ka", "s_qt",
                "s_zs", "s_st", "s_y", "s_dbg",
                "s_ag1", "s_ag2", "s_ag3",
                "s_pe", "s_ac", "s_dv", "s_po"]
    sems = {n: nc.alloc_semaphore(n) for n in semnames}
    cnt = {n: 0 for n in sems}
    ops = {e: [] for e in ["sync", "scalar", "gpsimd", "tensor", "vector"]}
    ESEM = {"tensor": "s_pe", "scalar": "s_ac", "vector": "s_dv", "gpsimd": "s_po"}

    def op(eng, fn, waits=(), incs=()):
        waits = [(s, v) for (s, v) in waits if v > 0]
        ops[eng].append((fn, waits, list(incs)))
        for s, a in incs:
            cnt[s] += a

    def dma(q, s, out, in_, waits=(), slow=False):
        if slow:
            fn = lambda e, out=out, in_=in_: e.dma_start(out=out, in_=in_, allow_slow_non_contiguous=True)
        else:
            fn = lambda e, out=out, in_=in_: e.dma_start(out=out, in_=in_)
        op(q, fn, waits, [(s, 16)])
        return cnt[s]

    def eop(eng, fn, waits=(), inc=1):
        op(eng, fn, waits, [(ESEM[eng], inc)] if inc else [])
        return cnt[ESEM[eng]]

    def mmop(fn, waits=(), inc=1):
        return eop("tensor", fn, waits, inc)

    def pe_fence():
        return eop("tensor", lambda t: t.nop(), [], inc=1)

    def coll(kind, s, ins, outs, waits=()):
        fn = lambda e, ins=ins, outs=outs: e.collective_compute(
            kind, ALU.bypass, replica_groups=[list(range(NCORES))],
            ins=[ins.opt()], outs=[outs.opt()])
        op("gpsimd", fn, waits, [(s, 1)])
        return cnt[s]

    a1i = ag1_in[0, :]

    dma("scalar", "s_sc", ident[:], ident_i)
    dma("scalar", "s_sc", eh[:], eh_i)
    dma("scalar", "s_sc", bqt[:], bqt_d)
    dma("scalar", "s_sc", bkt[:], bkt_d)
    dma("scalar", "s_sc", bvt[:], bvt_d)
    dma("scalar", "s_sc", b1t[:], b1t_d)
    dma("scalar", "s_sc", ones1f[:], ones1f_i)
    c_bw = dma("scalar", "s_sc", bwrow[:], bw_d)
    c_consts = c_bw
    for _e in ("tensor", "vector", "gpsimd"):
        op(_e, lambda e: e.nop(), [("s_sc", c_consts)], [])
    eop("vector", lambda v: v.memset(epsc[:], EPS))

    prep_cnt = 0
    for mi, wsrc in enumerate([wq_s, wk_s, wv_s]):
        dst = ag1_in[0, mi * MAT_SZ:(mi + 1) * MAT_SZ].rearrange("(p j) -> p j", j=512)
        prep_cnt = dma("gpsimd", "s_gpw", dst, wsrc)
    c_ag1 = coll("AllGather", "s_ag1", ag1_in, ag1_out, waits=[("s_gpw", prep_cnt)])

    if not want("ph1"):
        return _finish(nc, ops, sems, cnt, dbg_outs)
    xn_tr_done = [0] * 8
    for lt in range(8):
        wl = []
        if lt >= 2:
            wl.append(("s_pe", xn_tr_done[lt - 2]))
        c_xl = dma("sync", f"s_x{lt % 2}", xld[lt % 2][:], x_rows[lt * 128:(lt + 1) * 128, :], waits=wl)
        for i in range(8):
            wl2 = [(f"s_x{lt % 2}", c_xl)] if i == 0 else []
            eop("vector", lambda v, i=i, lt=lt: v.bn_stats(
                stats[:, i * 6:(i + 1) * 6], xld[lt % 2][:, i * 512:(i + 1) * 512]), wl2)
        c_st = cnt["s_dv"]
        c_agg = eop("vector", lambda v: v.bn_aggr(aggr[:], stats[:]), [("s_dv", c_st)])
        c_ln = eop("scalar", lambda s: s.activation(t128a[:], aggr[:, 1:2], AF.Ln, bias=epsc[:]),
                   [("s_dv", c_agg)])
        c_rs = eop("scalar", lambda s: s.activation(rstd[:], t128a[:], AF.Exp, scale=-0.5),
                   [("s_ac", c_ln)])
        c_xn = eop("vector", lambda v, lt=lt: v.tensor_scalar(
            xncast[:], xld[lt % 2][:], aggr[:, 0:1], rstd[:], ALU.subtract, ALU.mult),
            [("s_ac", c_rs), ("s_dv", c_agg), ("s_pe", xn_tr_done[lt - 1] if lt >= 1 else 0)])
        for kc in range(32):
            pt = PT[kc % 2]
            waits = [("s_dv", c_xn)] if kc == 0 else []
            if kc >= 2:
                waits.append(("s_dv", cnt["s_dv"]))
            def tr(t, kc=kc, pt=pt):
                return t.transpose(pt[:, 0:128], xncast[:, kc * 128:(kc + 1) * 128], ident[:])
            c_tr = mmop(tr, waits)
            def ev(v, pt=pt, kc=kc, lt=lt):
                return v.tensor_copy(xnT[:, kc * 1024 + lt * 128: kc * 1024 + (lt + 1) * 128],
                                     pt[:, 0:128])
            eop("vector", ev, [("s_pe", c_tr)])
        xn_tr_done[lt] = cnt["s_pe"]
    c_xnT_done = cnt["s_dv"]

    if "xnt" in dbg:
        o = dbg_tensor("xnt", [128, 32 * 1024], BF16)
        dma("gpsimd", "s_dbg", o, xnT[:], waits=[("s_dv", c_xnT_done)])

    def proj(mat_idx, evac_hook, seed_free, extra_first=()):
        pp_free = dict(seed_free)
        wb_free = [cnt["s_pe"], cnt["s_pe"]]
        c_wb = [0, 0]
        rot = 0
        last = {}
        for ocg in range(16):
            b = ocg % 2
            rank = ocg // 2
            col0 = (ocg % 2) * 256
            src = ag1_out[rank, mat_idx * MAT_SZ:(mat_idx + 1) * MAT_SZ] \
                .rearrange("(kc p j) -> p kc j", kc=32, p=128)[:, :, col0:col0 + 256]
            dstv = wbuf[b].rearrange("p (kc j) -> p kc j", kc=32)
            wl = [("s_pe", wb_free[b]), ("s_ag1", c_ag1)]
            if ocg < 2:
                wl.append(("s_dv", c_xnT_done))
                wl += list(extra_first)
            c_wb[b] = dma("sync", f"s_wb{b}", dstv, src, waits=wl)
            for oc2 in range(2):
                oc = ocg * 2 + oc2
                for ncc in range(2):
                    pp = P[2 + rot]
                    for kc in range(32):
                        waits = []
                        if kc == 0:
                            waits = [(f"s_wb{b}", c_wb[b]), pp_free[rot], ("s_dv", c_xnT_done)]
                        def mm(t, pp=pp, b=b, kc=kc, oc2=oc2, ncc=ncc,
                               first=(kc == 0), lastk=(kc == 31)):
                            return t.matmul(pp[:, 0:512],
                                            wbuf[b][:, kc * 256 + oc2 * 128: kc * 256 + oc2 * 128 + 128],
                                            xnT[:, kc * 1024 + ncc * 512: kc * 1024 + ncc * 512 + 512],
                                            start=first, stop=lastk, skip_group_check=True)
                        c_mm = mmop(mm, waits, inc=1 if kc == 31 else 0)
                    pp_free[rot] = evac_hook(oc, ncc, pp, c_mm)
                    last[rot] = pp_free[rot]
                    rot ^= 1
            wb_free[b] = cnt["s_pe"]
        return last

    if not want("vproj"):
        return _finish(nc, ops, sems, cnt, dbg_outs)
    def v_evac(oc, ncc, pp, c_mm):
        def ev(s, oc=oc, ncc=ncc, pp=pp):
            return s.activation(VT[:, oc * 1024 + ncc * 512: oc * 1024 + ncc * 512 + 512],
                                pp[:, 0:512], AF.Identity, bias=bvt[:, oc:oc + 1])
        c = eop("scalar", ev, [("s_pe", c_mm)])
        return ("s_ac", c)
    v_last = proj(2, v_evac, {0: ("s_dv", 0), 1: ("s_dv", 0)})
    c_VT_done = cnt["s_ac"]

    vrow_free = [0, 0]
    vr_tot = [0, 0]
    c_vwr = 0
    ev_hist2 = []
    for j in range(8):
        b = j % 2
        c_ones = eop("vector", lambda v, b=b: v.memset(
            vrow[b].rearrange("p (h c) -> p h c", c=257)[:, :, 256:257], 1.0),
            [(f"s_vr{b}", vrow_free[b])])
        for oc in range(32):
            gidx = j * 32 + oc
            pt = PT[oc % 2]
            waits = []
            if gidx == 0:
                waits.append(("s_ac", c_VT_done))
                waits.append(("s_pe", xn_tr_done[7]))
            if gidx >= 2:
                waits.append(("s_dv", ev_hist2[gidx - 2]))
            def tr(t, oc=oc, j=j, pt=pt):
                return t.transpose(pt[:, 0:128],
                                   VT[:, oc * 1024 + j * 128: oc * 1024 + (j + 1) * 128],
                                   ident[:])
            c_tr = mmop(tr, waits)
            h, sub = oc // 2, oc % 2
            colb = h * 257 + sub * 128
            evw = [("s_pe", c_tr)]
            if oc == 0:
                evw.append(("s_dv", c_ones))
            def ev(v, pt=pt, b=b, colb=colb):
                return v.tensor_copy(vrow[b][:, colb:colb + 128], pt[:, 0:128])
            ev_hist2.append(eop("vector", ev, evw))
        c_as = cnt["s_dv"]
        dst = agv_in[0, j * 128 * VROW: (j + 1) * 128 * VROW] \
            .rearrange("(p c) -> p c", p=128)
        c_vwr = dma("gpsimd", f"s_vr{b}", dst, vrow[b][:], waits=[("s_dv", c_as)])
        vrow_free[b] = c_vwr
        vr_tot[b] = c_vwr

    if not want("kproj"):
        return _finish(nc, ops, sems, cnt, dbg_outs)
    c_agv = coll("AllGather", "s_ag1", agv_in, agv_out,
                 waits=[("s_vr0", vr_tot[0]), ("s_vr1", vr_tot[1])])

    kst_free = [("s_vr0", vr_tot[0]), ("s_vr0", vr_tot[0])]
    kq_started = [False, False]
    kt_view = ag3_in[0, 0:KT_SZ].rearrange("(f m) -> f m", m=ROWS)
    k_state = {"c_kwr": 0}
    def k_evac(oc, ncc, pp, c_mm):
        b = (oc * 2 + ncc) % 2
        def ev(s, pp=pp, b=b, oc=oc):
            return s.activation(kstage[b][:], pp[:, 0:512], AF.Identity, bias=bkt[:, oc:oc + 1])
        c_ev = eop("scalar", ev, [("s_pe", c_mm), kst_free[b]])
        def sq(s, b=b, ncc=ncc):
            return s.activation(ktsq[:, ncc * 512:(ncc + 1) * 512], kstage[b][:], AF.Square)
        c_sq = eop("scalar", sq, [("s_ac", c_ev), ("s_pe", cnt["s_pe"])])
        h = oc // 2
        first = not kq_started[ncc]
        kq_started[ncc] = True
        def mq(t, h=h, ncc=ncc, first=first):
            return t.matmul(P[4 + ncc][0:16, 0:512], eh[:, h * 16:(h + 1) * 16],
                            ktsq[:, ncc * 512:(ncc + 1) * 512], start=first, stop=False,
                            skip_group_check=True)
        mmop(mq, [("s_ac", c_sq)])
        dstk = kt_view[oc * 128:(oc + 1) * 128, ncc * 512:(ncc + 1) * 512]
        k_state["c_kwr"] = dma("gpsimd", f"s_ks{b}", dstk, kstage[b][:], waits=[("s_ac", c_ev)])
        kst_free[b] = (f"s_ks{b}", k_state["c_kwr"])
        k_state[b] = k_state["c_kwr"]
        return ("s_ac", c_ev)
    k_last = proj(1, k_evac, v_last)
    c_kq1 = eop("vector", lambda v: v.tensor_reduce(t16a[:], P[4][0:16, 0:512], AX.X, ALU.max),
                [("s_pe", cnt["s_pe"])])
    c_kq2 = eop("vector", lambda v: v.tensor_reduce(t16b[:], P[5][0:16, 0:512], AX.X, ALU.max),
                [("s_dv", c_kq1)])
    c_kq3 = eop("vector", lambda v: v.tensor_tensor(kqm[:], t16a[:], t16b[:], ALU.max),
                [("s_dv", c_kq2)])
    c_kmax_wr = dma("gpsimd", "s_km", ag3_in[0, KMAX_OFF:KMAX_OFF + 16].rearrange("(k a) -> k a", a=1),
                    kqm[:], waits=[("s_dv", c_kq3)], slow=True)

    c_ag3 = coll("AllGather", "s_ag3", ag3_in, ag3_out,
                 waits=[("s_ks0", k_state.get(0, 0)), ("s_ks1", k_state.get(1, 0)),
                        ("s_km", c_kmax_wr)])

    c_c1 = dma("gpsimd", "s_cb0", ag2_in[0, 0:W1_SZ].rearrange("(p j) -> p j", j=FF), w1_s)
    c_c2 = dma("gpsimd", "s_cb1", ag2_in[0, W1_SZ:].rearrange("(p j) -> p j", j=D), w2_s)
    c_ag2 = coll("AllGather", "s_ag2", ag2_in, ag2_out,
                 waits=[("s_cb0", c_c1), ("s_cb1", c_c2)])
    cast_gate = [("s_cb0", c_c1), ("s_cb1", c_c2)]

    c_bdc = 0
    c_pe_after_k = cnt["s_pe"]
    for i in range(4):
        c_bdc = dma("scalar", "s_sc", bdc[i][:], bd_i[i],
                    waits=[("s_ks0", k_state.get(0, 0)), ("s_ks1", k_state.get(1, 0)),
                           ("s_pe", c_pe_after_k)])
    c_bwmm = mmop(lambda t: t.matmul(P[0][:, 0:48], ones1f[:], bwrow[:], start=True, stop=True,
                                     skip_group_check=True),
                  [("s_sc", c_bw), k_last[0], k_last[1]])
    c_bwrep = eop("vector", lambda v: v.tensor_copy(bwrep[:], P[0][:, 0:48]),
                  [("s_pe", c_bwmm)])

    bd_built = {}
    fold_done_at = {}
    bd_pool_at = {}
    def build_bd(h):
        b = h % 3
        wl = [("s_sc", c_bdc), ("s_dv", c_bwrep)]
        if h == 0:
            wl.append(("s_vr0", vr_tot[0]))
            wl.append(("s_vr1", vr_tot[1]))
        if h >= 1:
            wl.append(("s_po", bd_pool_at[h - 1]))
        if h >= 3:
            wl.append(("s_pe", fold_done_at[h - 3]))
        c1 = eop("vector", lambda v, h=h: v.tensor_scalar(
            bdt1[:], bdc[1][:], bwrep[:, 3 * h:3 * h + 1], None, ALU.mult), wl)
        c2 = eop("vector", lambda v, h=h: v.scalar_tensor_tensor(
            bdt2[:], bdc[2][:], bwrep[:, 3 * h + 1:3 * h + 2], bdc[0][:], ALU.mult, ALU.add), [])
        c3 = eop("gpsimd", lambda g, b=b: g.tensor_tensor(bdh[b][:], bdt1[:], bdt2[:], ALU.add),
                 [("s_dv", c2)])
        bd_pool_at[h] = c3
        c4 = eop("vector", lambda v, h=h, b=b: v.scalar_tensor_tensor(
            bdh[b][:], bdc[3][:], bwrep[:, 3 * h + 2:3 * h + 3], bdh[b][:], ALU.mult, ALU.add),
            [("s_po", c3)])
        bd_built[h] = c4

    build_bd(0)
    build_bd(1)
    qraw_ready = {}
    qq_started = [False, False]
    fold_evac = {0: ("s_dv", c_bwrep), 1: ("s_dv", 0)}

    def emit_fold(h):
        b = h % 3
        for Mc in range(2):
            for n2 in range(2):
                frot = (Mc * 2 + n2) % 2
                pf = P[frot]
                for t2 in range(2):
                    waits = []
                    if t2 == 0:
                        waits = [("s_ac", qraw_ready[h]), ("s_dv", bd_built[h]),
                                 fold_evac[frot]]
                    def mm(t_, h=h, b=b, Mc=Mc, n2=n2, t2=t2, pf=pf):
                        return t_.matmul(pf[:, 0:512],
                                         bdh[b][:, t2 * 256 + Mc * 128: t2 * 256 + Mc * 128 + 128],
                                         qtraw[b][:, t2 * 1024 + n2 * 512: t2 * 1024 + n2 * 512 + 512],
                                         start=(t2 == 0), stop=(t2 == 1), skip_group_check=True)
                    c_f = mmop(mm, waits, inc=1 if t2 == 1 else 0)
                def ev2(v, pf=pf, h=h, Mc=Mc, n2=n2):
                    o2 = (2 * h + Mc) * 1024 + n2 * 512
                    return v.tensor_copy(q2T[:, o2:o2 + 512], pf[:, 0:512])
                c_e2 = eop("vector", ev2, [("s_pe", c_f)])
                fold_evac[frot] = ("s_dv", c_e2)
                def sq(s, h=h, Mc=Mc, n2=n2):
                    o2 = (2 * h + Mc) * 1024 + n2 * 512
                    return s.activation(qtsq[:, n2 * 512:(n2 + 1) * 512],
                                        q2T[:, o2:o2 + 512], AF.Square)
                c_sq = eop("scalar", sq, [("s_dv", c_e2), ("s_pe", cnt["s_pe"])])
                first = not qq_started[n2]
                qq_started[n2] = True
                qw = [("s_ac", c_sq)]
                if first:
                    qw.append(("s_dv", c_kq3))
                def mq(t_, h=h, n2=n2, first=first):
                    return t_.matmul(P[4 + n2][0:16, 0:512], eh[:, h * 16:(h + 1) * 16],
                                     qtsq[:, n2 * 512:(n2 + 1) * 512],
                                     start=first, stop=False, skip_group_check=True)
                mmop(mq, qw)
        fold_done_at[h] = cnt["s_pe"]

    def q_evac(oc, ncc, pp, c_mm):
        h, tt_ = oc // 2, oc % 2
        b = h % 3
        def ev(s, pp=pp, b=b, tt_=tt_, ncc=ncc, oc=oc):
            return s.activation(qtraw[b][:, tt_ * 1024 + ncc * 512: tt_ * 1024 + ncc * 512 + 512],
                                pp[:, 0:512], AF.Identity, bias=bqt[:, oc:oc + 1])
        wl = [("s_pe", c_mm)]
        if h == 0:
            wl.append(("s_vr0", vr_tot[0]))
            wl.append(("s_vr1", vr_tot[1]))
        if h >= 3:
            wl.append(("s_pe", fold_done_at[h - 3]))
        c_ev = eop("scalar", ev, wl)
        if tt_ == 1 and ncc == 1:
            qraw_ready[h] = c_ev
            if h >= 1:
                emit_fold(h - 1)
            if h + 2 < NH:
                build_bd(h + 2)
            if h == NH - 1:
                emit_fold(h)
        return ("s_ac", c_ev)
    proj(0, q_evac, k_last)
    c_q_done = cnt["s_pe"]
    c_qq1 = eop("vector", lambda v: v.tensor_reduce(t16a[:], P[4][0:16, 0:512], AX.X, ALU.max),
                [("s_pe", c_q_done)])
    c_qq2 = eop("vector", lambda v: v.tensor_reduce(t16b[:], P[5][0:16, 0:512], AX.X, ALU.max),
                [("s_dv", c_qq1)])
    c_qq3 = eop("vector", lambda v: v.tensor_tensor(qm2[:], t16a[:], t16b[:], ALU.max),
                [("s_dv", c_qq2)])
    c_qmw = dma("gpsimd", "s_qm", qm_bounce, qm2[:], waits=[("s_dv", c_qq3)], slow=True)

    if "q2t" in dbg:
        o = dbg_tensor("q2t", [128, 32 * 1024], BF16)
        dma("gpsimd", "s_dbg", o, q2T[:], waits=[("s_dv", cnt["s_dv"]), ("s_pe", c_q_done)])
    if "ag3" in dbg:
        o = dbg_tensor("ag3", [1, AG3_N], BF16)
        dma("gpsimd", "s_dbg", o, ag3_in, waits=[("s_ag3", c_ag3)])

    if not want("cbias"):
        return _finish(nc, ops, sems, cnt, dbg_outs)
    c_kall = dma("gpsimd", "s_ka", kall.rearrange("a (r k) -> a r k", r=8),
                 ag3_out[:, KMAX_OFF:KMAX_OFF + 16].unsqueeze(0),
                 waits=[("s_ag3", c_ag3)], slow=True)
    c_qmT = dma("gpsimd", "s_qt", qmT.rearrange("a (k b) -> a k b", b=1), qm_bounce.unsqueeze(0),
                waits=[("s_qm", c_qmw)], slow=True)
    c_km = eop("vector", lambda v: v.tensor_reduce(
        km2[:], kall.rearrange("a (r h) -> a h r", r=8), AX.X, ALU.max),
        [("s_ka", c_kall)])
    c_pr = eop("vector", lambda v: v.tensor_tensor(csc1[:], qmT[:], km2[:], ALU.mult),
               [("s_qt", c_qmT), ("s_dv", c_km)])
    c_cl = eop("scalar", lambda s: s.activation(csc2[:], csc1[:], AF.Ln, bias=epsc[0:1, :]),
               [("s_dv", c_pr)])
    c_ce = eop("scalar", lambda s: s.activation(csc1[:], csc2[:], AF.Exp, scale=0.5),
               [("s_ac", c_cl)])
    c_cbmm = mmop(lambda t: t.matmul(P[2][:, 0:16], ones1f[:], csc1[0:1, :], start=True, stop=True,
                                     skip_group_check=True),
                  [("s_ac", c_ce), fold_evac[0]])
    c_cbias = eop("vector", lambda v: v.tensor_scalar(
        cbias[:], P[2][0:128, 0:16], -1.0, None, ALU.mult), [("s_pe", c_cbmm)])

    if not want("attn"):
        return _finish(nc, ops, sems, cnt, dbg_outs)
    def av_done_dictres(ref):
        if ref is None:
            return 0
        d, k = ref
        return d[k]

    ACC = [PC, PD, PT[0].bitcast(F32), PT[1].bitcast(F32)]
    ktb_free = [0, 0]
    vb_free = [None, None]
    c_att_done_half = [0, 0]
    c_last_exp = 0
    pacc_gp_gate = []
    for hf in range(2):
        pacc_drained = max(cnt["s_dv"], c_cbias)
        for h in range(NH):
            c_ktl = [0] * 4
            c_vl = [0] * 4
            exp_cnt_at = {}
            av_done_at = {}
            st_at = {}
            c_st = c_av = 0
            state = {"c_av": 0}
            def emit_av(pv):
                bv_ = (pv // 8) % 2
                avw = [("s_ac", exp_cnt_at[pv]), (f"s_vb{bv_}", c_vl[pv // 8])]
                for gi in range(2):
                    mt = (pv % 8) * 2 + gi
                    g = (pv // 8) * 16 + mt
                    for lc in range(4):
                        def mm2(t, g=g, bv_=bv_, mt=mt, lc=lc, gi=gi, pv=pv,
                                first=(g == 0), lastg=(g == 63)):
                            return t.matmul(ACC[lc][:, 0:257],
                                            expb[pv % 2][:, gi * 512 + lc * 128: gi * 512 + lc * 128 + 128],
                                            vb[bv_][:, mt * 257:(mt + 1) * 257],
                                            start=first, stop=lastg, skip_group_check=True)
                        state["c_av"] = mmop(mm2, avw if (gi == 0 and lc == 0) else [],
                                             inc=1 if (gi == 1 and lc == 3) else 0)
                av_done_at[pv] = state["c_av"]
            for c4 in range(4):
                b = c4 % 2
                extra = []
                if hf == 0 and h == 0 and c4 < 2:
                    extra = cast_gate + [("s_pe", c_q_done)]
                wl = [("s_pe", ktb_free[b]), ("s_ag3", c_ag3)] + extra
                for ri in range(2):
                    rank = c4 * 2 + ri
                    for ft in range(2):
                        f0 = h * 256 + ft * 128
                        src = ag3_out[rank, f0 * ROWS:(f0 + 128) * ROWS] \
                            .rearrange("(p m) -> p m", p=128)
                        dstk = ktb[b][:, ft * 2048 + ri * 1024: ft * 2048 + ri * 1024 + 1024]
                        c_ktl[c4] = dma("sync", f"s_kt{b}", dstk, src, waits=wl)
                        wl = []
                vgate = av_done_dictres(vb_free[b])
                wl2 = [("s_pe", vgate), ("s_ag1", c_agv)] + extra
                for ri in range(2):
                    rank = c4 * 2 + ri
                    src = agv_out[rank, :] \
                        .rearrange("(mt p c) -> p mt c", mt=8, p=128)[:, :, h * 257:(h + 1) * 257]
                    dstv = vb[b].rearrange("p (mt c) -> p mt c", c=257)[:, ri * 8:(ri + 1) * 8, :]
                    c_vl[c4] = dma("sync", f"s_vb{b}", dstv, src, waits=wl2)
                    wl2 = []
                for prc in range(8):
                    pr = c4 * 8 + prc
                    pst = [PA, PBt][pr % 2]
                    waits = []
                    if pr == 0:
                        waits = [(f"s_kt{b}", c_ktl[c4]), ("s_dv", pacc_drained)]
                        if hf == 1:
                            waits += pacc_gp_gate
                    elif prc == 0:
                        waits = [(f"s_kt{b}", c_ktl[c4])]
                    if pr >= 2:
                        waits.append(("s_ac", exp_cnt_at[pr - 2]))
                    for gi in range(2):
                        mt = prc * 2 + gi
                        for ft in range(2):
                            def mm(t, b=b, ft=ft, mt=mt, h=h, hf=hf, pst=pst, gi=gi):
                                return t.matmul(pst[:, gi * 512:(gi + 1) * 512],
                                                ktb[b][:, ft * 2048 + mt * 128: ft * 2048 + mt * 128 + 128],
                                                q2T[:, (2 * h + ft) * 1024 + hf * 512:
                                                    (2 * h + ft) * 1024 + hf * 512 + 512],
                                                start=(ft == 0), stop=(ft == 1), skip_group_check=True)
                            c_st = mmop(mm, waits if (gi == 0 and ft == 0) else [],
                                        inc=1 if (gi == 1 and ft == 1) else 0)
                    st_at[pr] = c_st
                    ew = [("s_pe", c_st)]
                    if pr >= 2:
                        ew.append(("s_pe", av_done_at[pr - 2]))
                    def ex(s, pr=pr, h=h, pst=pst):
                        return s.activation(expb[pr % 2][:], pst[:, 0:1024], AF.Exp,
                                            bias=cbias[:, h:h + 1])
                    exp_cnt_at[pr] = eop("scalar", ex, ew)
                    if pr >= 1:
                        emit_av(pr - 1)
                    if pr == 31:
                        emit_av(31)
                ktb_free[b] = cnt["s_pe"]
                vb_free[b] = (av_done_at, c4 * 8 + 7)
            c_last_exp = exp_cnt_at[31]
            c_rc = 0
            for lc in range(4):
                c_rc = eop("vector", lambda v, lc=lc: v.reciprocal(
                    rec4[:, lc:lc + 1], ACC[lc][:, 256:257]), [("s_pe", av_done_at[31])])
            for lc in range(4):
                def sc(v, lc=lc, h=h):
                    return v.tensor_scalar(attnout[:, lc * D + h * 256: lc * D + h * 256 + 256],
                                           ACC[lc][:, 0:256], rec4[:, lc:lc + 1], None, ALU.mult)
                eop("vector", sc, [("s_dv", c_rc)])
            pacc_drained = cnt["s_dv"]
        c_att_done_half[hf] = cnt["s_dv"]

        if "att" in dbg and hf == 0:
            o = dbg_tensor("att", [128, 4 * D], F32)
            dma("gpsimd", "s_dbg", o, attnout[:], waits=[("s_dv", c_att_done_half[0])])

        c_z_prev = 0
        c_stagedma_prev = 0
        xn3_tr_prev = 0
        for lt4 in range(4):
            lt = hf * 4 + lt4
            aosl = attnout[:, lt4 * D:(lt4 + 1) * D]
            c_xr = dma("scalar", "s_sc", xre[:], x_rows[lt * 128:(lt + 1) * 128, :],
                       waits=[("s_dv", c_z_prev if lt4 > 0 else c_att_done_half[hf])])
            for i in range(8):
                wl3 = [("s_dv", c_att_done_half[hf])] if i == 0 else []
                eop("vector", lambda v, i=i, aosl=aosl: v.bn_stats(
                    stats[:, i * 6:(i + 1) * 6], aosl[:, i * 512:(i + 1) * 512]), wl3)
            c_st2 = cnt["s_dv"]
            c_ag2v = eop("vector", lambda v: v.bn_aggr(aggr[:], stats[:]), [("s_dv", c_st2)])
            c_ln2 = eop("scalar", lambda s: s.activation(t128a[:], aggr[:, 1:2], AF.Ln, bias=epsc[:]),
                        [("s_dv", c_ag2v)])
            c_rs2 = eop("scalar", lambda s: s.activation(rstd[:], t128a[:], AF.Exp, scale=-0.5),
                        [("s_ac", c_ln2)])
            c_mr = eop("vector", lambda v: v.tensor_tensor(t128b[:], aggr[:, 0:1], rstd[:], ALU.mult),
                       [("s_ac", c_rs2), ("s_dv", c_ag2v)])
            c_x2 = eop("vector", lambda v: v.tensor_scalar(
                xre[:], xre[:], t128b[:], None, ALU.subtract),
                [("s_dv", c_mr), ("s_sc", c_xr)])
            c_z = eop("vector", lambda v, aosl=aosl: v.scalar_tensor_tensor(
                aosl[:], aosl[:], rstd[:], xre[:], ALU.mult, ALU.add),
                [("s_dv", c_x2)])
            c_z_prev = c_z
            c_zs = dma("gpsimd", "s_zs", z_spill[lt * 128:(lt + 1) * 128, :], aosl[:],
                       waits=[("s_dv", c_z)])
            for i in range(8):
                eop("vector", lambda v, i=i, aosl=aosl: v.bn_stats(
                    stats[:, i * 6:(i + 1) * 6], aosl[:, i * 512:(i + 1) * 512]),
                    [("s_dv", c_z)] if i == 0 else [])
            c_st3 = cnt["s_dv"]
            c_ag3v = eop("vector", lambda v: v.bn_aggr(aggr[:], stats[:]), [("s_dv", c_st3)])
            c_ln3 = eop("scalar", lambda s: s.activation(t128a[:], aggr[:, 1:2], AF.Ln, bias=epsc[:]),
                        [("s_dv", c_ag3v)])
            c_rs3 = eop("scalar", lambda s: s.activation(rstd[:], t128a[:], AF.Exp, scale=-0.5),
                        [("s_ac", c_ln3)])
            c_x3 = eop("vector", lambda v, aosl=aosl: v.tensor_scalar(
                xn3b[:], aosl[:], aggr[:, 0:1], rstd[:], ALU.subtract, ALU.mult),
                [("s_ac", c_rs3), ("s_dv", c_ag3v), ("s_pe", xn3_tr_prev)])
            for kc in range(32):
                pt = PT[kc % 2]
                waits = [("s_dv", c_x3)] if kc == 0 else []
                if kc == 0:
                    waits.append(("s_st", c_stagedma_prev))
                if kc >= 2:
                    waits.append(("s_dv", cnt["s_dv"]))
                def tr(t, kc=kc, pt=pt):
                    return t.transpose(pt[:, 0:128], xn3b[:, kc * 128:(kc + 1) * 128], ident[:])
                c_tr = mmop(tr, waits)
                def ev(v, pt=pt, kc=kc):
                    return v.tensor_copy(xn3stage[:, kc * 128:(kc + 1) * 128], pt[:, 0:128])
                eop("vector", ev, [("s_pe", c_tr)])
            xn3_tr_prev = cnt["s_pe"]
            c_sta = cnt["s_dv"]
            dst3 = x3t_spill.rearrange("(ft p) m -> p ft m", p=128)[:, :, lt * 128:(lt + 1) * 128]
            srcv = xn3stage.rearrange("p (ft l) -> p ft l", ft=32)
            c_stagedma_prev = dma("gpsimd", "s_st", dst3, srcv, waits=[("s_dv", c_sta)])
        pacc_gp_gate = [("s_st", cnt["s_st"]), ("s_zs", cnt["s_zs"])]
    c_3b_done = None
    c_3b_dv = cnt["s_dv"]
    gate_3b = [("s_st", cnt["s_st"]), ("s_zs", cnt["s_zs"])]
    if "z" in dbg:
        o = dbg_tensor("z", [ROWS, D], F32)
        dma("gpsimd", "s_dbg", o, z_spill, waits=gate_3b)

    if not want("ffn"):
        return _finish(nc, ops, sems, cnt, dbg_outs)
    FCH = 4
    PB = [PA[:, 0:512], PA[:, 512:1024], PBt[:, 0:512], PBt[:, 512:1024],
          PC[:, 0:512], PD[:, 0:512],
          PT[0].bitcast(F32)[:, 0:512], PT[1].bitcast(F32)[:, 0:512]]

    srcx = x3t_spill.rearrange("(ft p) m -> p ft m", p=128)
    dstx = f_xn3T.rearrange("p (ft l) -> p ft l", ft=32)
    c_x3f = dma("sync", "s_xq0", dstx, srcx, waits=[("s_pe", cnt["s_pe"])] + gate_3b)

    w1b_free = [0, 0]
    w2_free = {0: 0, 1: 0}
    pb_evac = {0: ("s_ac", c_last_exp), 1: ("s_ac", c_last_exp),
               2: ("s_dv", c_att_done_half[1]), 3: ("s_dv", c_att_done_half[1]),
               4: ("s_dv", c_att_done_half[1]), 5: ("s_dv", c_att_done_half[1]),
               6: ("s_dv", c_3b_dv), 7: ("s_dv", c_3b_dv)}
    gelu_at = {}
    c_os = 0
    OS = "s_gpl"
    os_total_prev = 0
    for Fch in range(FCH):
        for fg in range(16):
            bw_ = fg % 2
            wlw1 = [("s_pe", w1b_free[bw_]), ("s_ag2", c_ag2)]
            c_w1 = 0
            for rk in range(8):
                src1 = ag2_out[rk, 0:W1_SZ].rearrange("(kcl p f) -> p kcl f", kcl=4, p=128)[
                    :, :, Fch * 4096 + fg * 256: Fch * 4096 + (fg + 1) * 256]
                dst1 = f_w1b[bw_].rearrange("p (kc j) -> p kc j", kc=32)[:, rk * 4:(rk + 1) * 4, :]
                c_w1 = dma("sync", f"s_w1{bw_}", dst1, src1, waits=wlw1)
                wlw1 = []
            for f2 in range(2):
                fc = fg * 2 + f2
                gfc = Fch * 32 + fc
                for nc2 in range(2):
                    rot = (fc * 2 + nc2) % 2
                    pp = P[rot]
                    for kc in range(32):
                        waits = []
                        if kc == 0:
                            waits = [(f"s_w1{bw_}", c_w1), (f"s_xq0", c_x3f), pb_evac[rot]]
                            if (fc * 2 + nc2) >= 2:
                                waits.append(("s_ac", gelu_at[(Fch, fc * 2 + nc2 - 2)]))
                        def mm(t, pp=pp, bw_=bw_, kc=kc, f2=f2, nc2=nc2):
                            return t.matmul(pp[:, 0:512],
                                            f_w1b[bw_][:, kc * 256 + f2 * 128: kc * 256 + f2 * 128 + 128],
                                            f_xn3T[:, kc * 1024 + nc2 * 512: kc * 1024 + nc2 * 512 + 512],
                                            start=(kc == 0), stop=(kc == 31), skip_group_check=True)
                        c_m1 = mmop(mm, waits, inc=1 if kc == 31 else 0)
                    def ge(s, pp=pp, fc=fc, nc2=nc2, gfc=gfc):
                        return s.activation(f_hT[:, fc * 1024 + nc2 * 512: fc * 1024 + nc2 * 512 + 512],
                                            pp[:, 0:512], AF.Gelu_apprx_tanh,
                                            bias=b1t[:, gfc:gfc + 1])
                    gelu_at[(Fch, fc * 2 + nc2)] = eop("scalar", ge, [("s_pe", c_m1)])
            w1b_free[bw_] = c_m1
        c_gelu_all = gelu_at[(Fch, 63)]
        for ncd in range(8):
            w2_free_pe = cnt["s_pe"]
            c_m2 = 0
            for kcg in range(8):
                kc0 = kcg * 4
                frow = Fch * 4096 + kc0 * 128
                rank = frow // 2048
                rloc = frow % 2048
                slot = kcg % 2
                src2 = ag2_out[rank, W1_SZ + rloc * D: W1_SZ + (rloc + 512) * D] \
                    .rearrange("(kk p d) -> p kk d", kk=4, p=128)[:, :, ncd * 512:(ncd + 1) * 512]
                dst2 = f_w2r.rearrange("p (t j) -> p t j", t=8)[:, slot * 4:(slot + 1) * 4, :]
                wl4 = [("s_pe", w2_free[slot]), ("s_ag2", c_ag2)]
                c_w2l = dma("sync", f"s_w2{slot}", dst2, src2, waits=wl4)
                for kk in range(4):
                    kc = kc0 + kk
                    for Mc in range(8):
                        waits = []
                        if kk == 0 and Mc == 0:
                            waits = [(f"s_w2{slot}", c_w2l)]
                        if kc == 0:
                            if ncd == 0 and Mc == 0:
                                waits.append(("s_ac", c_gelu_all))
                            waits.append(pb_evac[Mc])
                        def mm(t, kc=kc, Mc=Mc, slot=slot, kk=kk):
                            return t.matmul(PB[Mc],
                                            f_hT[:, kc * 1024 + Mc * 128: kc * 1024 + Mc * 128 + 128],
                                            f_w2r[:, (slot * 4 + kk) * 512:(slot * 4 + kk) * 512 + 512],
                                            start=(kc == 0), stop=(kc == 31), skip_group_check=True)
                        c_m2 = mmop(mm, waits, inc=1 if (kk == 3 and Mc == 7) else 0)
                w2_free[slot] = c_m2
            for Mc in range(8):
                def ev(v, Mc=Mc):
                    return v.tensor_copy(f_stage[:, Mc * 512:(Mc + 1) * 512], PB[Mc])
                wl5 = [("s_pe", c_m2), (OS, c_os)]
                pb_evac[Mc] = ("s_dv", eop("vector", ev, wl5))
            c_ev8 = cnt["s_dv"]
            srcs = f_stage.rearrange("p (Mc d) -> p Mc d", Mc=8)
            dsts = out_spill.rearrange("(Mc p) d -> p Mc d", p=128)[:, :, ncd * 512:(ncd + 1) * 512]
            wl6 = [("s_dv", c_ev8)]
            if ncd == 0:
                wl6.append((OS, os_total_prev))
            if Fch == 0:
                c_os = dma("gpsimd", OS, dsts, srcs, waits=wl6)
            else:
                def fnadd(e, dsts=dsts, srcs=srcs):
                    return e.dma_start(out=dsts, in_=srcs, accum_op=ALU.add)
                op("gpsimd", fnadd, wl6, [(OS, 16)])
                c_os = cnt[OS]
        os_total_prev = c_os

    c_b2l = dma("scalar", "s_sc", f_out[0:1, :], b2_d,
                waits=[("s_pe", w1b_free[0]), ("s_pe", w1b_free[1])])
    b2ev = {}
    c_b2mm = 0
    for i in range(8):
        wlb = [("s_sc", c_b2l), pb_evac[i % 2]]
        if i >= 2:
            wlb.append(("s_dv", b2ev[i - 2]))
        def mmb(t, i=i):
            return t.matmul(P[i % 2][:, 0:512], ones1f[:], f_out[0:1, i * 512:(i + 1) * 512],
                            start=True, stop=True, skip_group_check=True)
        c_b2mm = mmop(mmb, wlb)
        def evb(v, i=i):
            return v.tensor_copy(f_b2rep[:, i * 512:(i + 1) * 512], P[i % 2][:, 0:512])
        b2ev[i] = eop("vector", evb, [("s_pe", c_b2mm)])
        pb_evac[i % 2] = ("s_dv", b2ev[i])
    c_b2rep = cnt["s_dv"]

    zb_gate = ("s_pe", c_b2mm)
    for lt in range(8):
        c_ol = dma("scalar", "s_sc", f_out[:], out_spill[lt * 128:(lt + 1) * 128, :],
                   waits=[(OS, c_os), ("s_pe", c_b2mm),
                          ("s_dv", cnt["s_dv"] if lt > 0 else c_b2rep)])
        c_ob = eop("vector", lambda v: v.tensor_tensor(f_out[:], f_out[:], f_b2rep[:], ALU.add),
                   [("s_sc", c_ol), ("s_dv", c_b2rep)])
        for i in range(8):
            eop("vector", lambda v, i=i: v.bn_stats(
                stats[:, i * 6:(i + 1) * 6], f_out[:, i * 512:(i + 1) * 512]),
                [("s_dv", c_ob)] if i == 0 else [])
        c_st4 = cnt["s_dv"]
        c_ag4 = eop("vector", lambda v: v.bn_aggr(aggr[:], stats[:]), [("s_dv", c_st4)])
        c_ln4 = eop("scalar", lambda s: s.activation(t128a[:], aggr[:, 1:2], AF.Ln, bias=epsc[:]),
                    [("s_dv", c_ag4)])
        c_rs4 = eop("scalar", lambda s: s.activation(rstd[:], t128a[:], AF.Exp, scale=-0.5),
                    [("s_ac", c_ln4)])
        c_zl = dma("scalar", "s_sc", zb[:], z_spill[lt * 128:(lt + 1) * 128, :],
                   waits=[zb_gate, ("s_ac", c_rs4)])
        c_mr4 = eop("vector", lambda v: v.tensor_tensor(t128b[:], aggr[:, 0:1], rstd[:], ALU.mult),
                    [("s_ac", c_rs4), ("s_dv", c_ag4)])
        c_z4 = eop("vector", lambda v: v.tensor_scalar(
            zb[:], zb[:], t128b[:], None, ALU.subtract),
            [("s_dv", c_mr4), ("s_sc", c_zl)])
        c_fin = eop("vector", lambda v: v.scalar_tensor_tensor(
            zb[:], f_out[:], rstd[:], zb[:], ALU.mult, ALU.add),
            [("s_dv", c_z4)])
        c_y = dma("scalar", "s_y", y_rows[lt * 128:(lt + 1) * 128, :], zb[:],
                  waits=[("s_dv", c_fin)])
        zb_gate = ("s_y", c_y)
    if "ht" in dbg:
        o = dbg_tensor("ht", [128, 128 * 256], BF16)
        dma("gpsimd", "s_dbg", o, hT[:], waits=[("s_ac", cnt["s_ac"])])

    return _finish(nc, ops, sems, cnt, dbg_outs)


def _finish(nc, ops, sems, cnt, dbg_outs):
    ops["gpsimd"].append((lambda g: g.nop(),
       [(s, v) for s, v in cnt.items() if v > 0], []))
    with nc.Block() as blk:
        def emit(e, eng):
            for fn, waits, incs in ops[eng]:
                for s, v in waits:
                    e.wait_ge(sems[s], v)
                inst = fn(e)
                for s, a in incs:
                    inst = inst.then_inc(sems[s], a)
        blk.sync(lambda e: emit(e, "sync"))
        blk.scalar(lambda e: emit(e, "scalar"))
        blk.gpsimd(lambda e: emit(e, "gpsimd"))
        blk.tensor(lambda e: emit(e, "tensor"))
        blk.vector(lambda e: emit(e, "vector"))
    return nc, dbg_outs


def _host_inputs(inp):
    x = np.ascontiguousarray(np.asarray(inp["L0_attn_x"], dtype=np.float32).reshape(L, D))
    wq = np.asarray(inp["L0_attn_Wq"], dtype=np.float32)
    wk = np.asarray(inp["L0_attn_Wk"], dtype=np.float32)
    wv = np.asarray(inp["L0_attn_Wv"], dtype=np.float32)
    w1 = np.asarray(inp["L0_ffn_inter_w"], dtype=np.float32)
    w2 = np.asarray(inp["L0_ffn_output_w"], dtype=np.float32)
    bq = np.asarray(inp["L0_attn_bq"], dtype=np.float32)
    bk = np.asarray(inp["L0_attn_bk"], dtype=np.float32)
    bv = np.asarray(inp["L0_attn_bv"], dtype=np.float32)
    b1 = np.asarray(inp["L0_ffn_inter_b"], dtype=np.float32)
    b2 = np.asarray(inp["L0_ffn_output_b"], dtype=np.float32)
    bw = np.asarray(inp["L0_attn_bias_weight"], dtype=np.float32)
    bqt = np.ascontiguousarray(bq.reshape(32, 128).T)
    bkt = np.ascontiguousarray(bk.reshape(32, 128).T)
    bvt = np.ascontiguousarray(bv.reshape(32, 128).T)
    b1t = np.ascontiguousarray(b1.reshape(128, 128).T)
    maps = []
    for c in range(NCORES):
        maps.append({
            "x_rows": np.ascontiguousarray(x[c * ROWS:(c + 1) * ROWS]),
            "wq_s": np.ascontiguousarray(wq[c * 512:(c + 1) * 512].T),
            "wk_s": np.ascontiguousarray(wk[c * 512:(c + 1) * 512].T),
            "wv_s": np.ascontiguousarray(wv[c * 512:(c + 1) * 512].T),
            "w1_s": np.ascontiguousarray(w1[c * 512:(c + 1) * 512]),
            "w2_s": np.ascontiguousarray(w2[c * 2048:(c + 1) * 2048]),
            "bqt": bqt, "bkt": bkt, "bvt": bvt, "b1t": b1t,
            "b2row": np.ascontiguousarray(b2.reshape(1, D)),
            "bwrow": np.ascontiguousarray(bw.reshape(1, 48)),
        })
    return maps


def kernel(**inputs):
    nc, _ = build_nc()
    in_maps = _host_inputs(inputs)
    res = run_bass_kernel_spmd(nc, in_maps, core_ids=list(range(NCORES)))
    out = np.concatenate([r["y_rows"] for r in res.results], axis=0)
    return out.reshape(1, L, D)
